# version 11
# baseline (speedup 1.0000x reference)
import numpy as np
from contextlib import ExitStack

import concourse.bass as bass
import concourse.mybir as mybir
from concourse.alu_op_type import AluOpType
from concourse.bass_utils import run_bass_kernel_spmd

F32 = mybir.dt.float32
BF16 = mybir.dt.bfloat16
FP8 = mybir.dt.float8e4
AF = mybir.ActivationFunctionType
AX = mybir.AxisListType

S = 4096
TO = 1024
H = 2048
II = 8192
HC = 16
EPS = float(np.float32(1e-5))
SCL = float(np.float32(1.0) / np.float32(np.sqrt(np.float32(128.0))))
AMAX_FLOOR = float(np.float32(448e-30))
INV224 = float(np.float32(1.0) / np.float32(224.0))

PASSES = [[2 * i, 2 * i + 1] for i in range(8)]
GLEN = [len(p) for p in PASSES]
PT0 = [sum(g * 32 for g in GLEN[:p]) for p in range(9)]
GH0 = [sum(GLEN[:p]) for p in range(9)]

_CACHE = {}


class Sem:
    __slots__ = ("h", "n")

    def __init__(self, h):
        self.h = h
        self.n = 0

    def dma(self, eng, out, in_, **kw):
        eng.dma_start(out=out, in_=in_, **kw).then_inc(self.h, 16)
        self.n += 16
        return self.n

    def pdma(self, eng, out, in_, **kw):
        if self.n:
            eng.wait_ge(self.h, self.n)
        return self.dma(eng, out, in_, **kw)

    def tick(self, instr):
        instr.then_inc(self.h, 1)
        self.n += 1
        return self.n


def build_program():
    nc = bass.Bass("TRN2", target_bir_lowering=False, debug=False)
    es = ExitStack()

    def sem(name):
        return Sem(es.enter_context(nc.semaphore(name)))

    def sb(stack, name, shape, dt):
        return stack.enter_context(nc.sbuf_tensor(name, shape, dt))

    xb = nc.dram_tensor("xb", [S, H], F32, kind="ExternalInput").ap()
    xo = nc.dram_tensor("xo", [TO, H], F32, kind="ExternalInput").ap()
    wq = nc.dram_tensor("wq", [H, H], F32, kind="ExternalInput").ap()
    wk = nc.dram_tensor("wk", [H, H], F32, kind="ExternalInput").ap()
    wv = nc.dram_tensor("wv", [H, H], F32, kind="ExternalInput").ap()
    wo = nc.dram_tensor("wo", [H, H], F32, kind="ExternalInput").ap()
    bias4 = nc.dram_tensor("bias4", [4, H], F32, kind="ExternalInput").ap()
    gw = nc.dram_tensor("gw", [II, H], F32, kind="ExternalInput").ap()
    uw = nc.dram_tensor("uw", [II, H], F32, kind="ExternalInput").ap()
    dw = nc.dram_tensor("dw", [H, II], F32, kind="ExternalInput").ap()
    ident = nc.dram_tensor("ident", [128, 128], F32, kind="ExternalInput").ap()
    out = nc.dram_tensor("out", [TO, H], F32, kind="ExternalOutput").ap()

    kT_s = nc.dram_tensor("kT_s", [H, S], BF16, kind="Internal").ap()
    v_s = nc.dram_tensor("v_s", [S, H], BF16, kind="Internal").ap()
    gT_s = nc.dram_tensor("gT_s", [II, H], BF16, kind="Internal").ap()
    uT_s = nc.dram_tensor("uT_s", [II, H], BF16, kind="Internal").ap()
    dT_s = nc.dram_tensor("dT_s", [H, II], BF16, kind="Internal").ap()

    ps = [es.enter_context(nc.psum_tensor(f"ps{i}", [128, 512], F32))
          for i in range(8)]

    ident_sb = sb(es, "ident_sb", [128, 128], F32)
    ones_row = sb(es, "ones_row", [1, 128], F32)
    ones_bf = sb(es, "ones_bf", [1, 512], BF16)
    ones_col = sb(es, "ones_col", [128, 1], F32)
    ones_cbf = sb(es, "ones_cbf", [128, 1], BF16)
    st6 = [sb(es, f"st6_{p}", [128, 24], F32) for p in range(2)]
    ag = [sb(es, f"ag{p}", [128, 2], F32) for p in range(2)]
    sml = [sb(es, f"sml{p}", [128, 8], F32) for p in range(2)]
    q8a = sb(es, "q8a", [128, H], FP8)
    q8b = sb(es, "q8b", [128, H], FP8)
    amq2 = sb(es, "amq2", [128, 128], F32)
    s2q2 = sb(es, "s2q2", [128, 128], F32)
    ivq2 = sb(es, "ivq2", [128, 128], F32)

    prepS = ExitStack()
    wwb = [sb(prepS, f"wwb{p}", [128, H], F32) for p in range(2)]
    gdqb = sb(prepS, "gdqb", [128, H], BF16)
    udqb = sb(prepS, "udqb", [128, H], BF16)
    gbuf = [sb(prepS, f"gbuf{p}", [128, H], BF16) for p in range(2)]

    biasS = ExitStack()
    bias_sb = [sb(biasS, f"bias_sb{i}", [1, H], BF16) for i in range(3)]
    xbufS = ExitStack()
    xbuf = [sb(xbufS, f"xbuf{p}", [128, H], F32) for p in range(2)]
    AVTs = ExitStack()
    AVT = sb(AVTs, "AVT", [128, HC, TO], BF16)
    QTs = ExitStack()
    QT = sb(QTs, "QT", [128, HC, TO], BF16)
    xnbfS = ExitStack()
    xnbf = [sb(xnbfS, f"xnbf{p}", [128, H], BF16) for p in range(2)]

    cst = sem("cst")
    xld = [sem(f"xld{p}") for p in range(2)]
    vstat = sem("vstat")
    snorm = sem("snorm")
    tdone = sem("tdone")
    tprep = sem("tprep")
    kvwld = [sem(f"kvwld{p}") for p in range(2)]
    qwld = [sem(f"qwld{p}") for p in range(2)]
    kmm = sem("kmm")
    vmm = sem("vmm")
    qmm = sem("qmm")
    kev = sem("kev")
    vev = sem("vev")
    qev = sem("qev")
    kst = [sem(f"kst{p}") for p in range(2)]
    vst = [sem(f"vst{p}") for p in range(2)]
    krld = [sem(f"krld{p}") for p in range(2)]
    vcld = [sem(f"vcld{p}") for p in range(2)]
    smm = sem("smm")
    sexp = sem("sexp")
    ptav = sem("ptav")
    vzf = sem("vzf")
    bcmm = sem("bcmm")
    oev = sem("oev")
    wowld = sem("wowld")
    p3mm = sem("p3mm")
    p3ev = sem("p3ev")
    ost = [sem(f"ost{p}") for p in range(2)]
    xq = sem("xq")
    wgld = [sem(f"wgld{p}") for p in range(2)]
    gq = sem("gq")
    wsst = sem("wsst")
    gsld = [sem(f"gsld{p}") for p in range(2)]
    p4mm = sem("p4mm")
    hsil = sem("hsil")
    hq = sem("hq")
    dsld = [sem(f"dsld{p}") for p in range(2)]
    p4dpc = sem("p4dpc")
    fts = sem("fts")
    ttr = sem("ttr")
    tbe = sem("tbe")
    accst = [sem(f"accst{p}") for p in range(2)]

    blk = es.enter_context(nc.Block())

    def dmat(sy, dsem, out_ap, in_ap):
        if tdone.n:
            sy.wait_ge(tdone.h, tdone.n)
        if tprep.n:
            sy.wait_ge(tprep.h, tprep.n)
        sy.dma_start_transpose(out_ap, in_ap).then_inc(dsem.h, 16)
        dsem.n += 16
        return dsem.n

    @blk.gpsimd
    def _(g):
        for i in range(3):
            cst.dma(g, bias_sb[i][:], bias4[i:i + 1, :])
        cst.dma(g, ident_sb[:], ident)

    @blk.vector
    def _(v):
        v.memset(ones_row[:], 1.0)
        v.memset(ones_bf[:], 1.0)
        v.memset(ones_col[:], 1.0)
        v.memset(ones_cbf[:], 1.0)
        v.drain()
        vstat.tick(v.engine_nop())
    ONES_AT = vstat.n

    XLD_AT = {}

    def ln_gpsimd(g, i, src, row0, extra=None, bufs=None):
        bufs = bufs or xbuf
        par = i % 2
        if extra is not None:
            extra(g)
        if i >= 2:
            g.wait_ge(snorm.h, 2 * (i - 2) + 2)
        XLD_AT[i] = xld[par].dma(g, bufs[par][:], src[row0:row0 + 128, :])

    def ln_vec_ab(v, i, bufs=None):
        bufs = bufs or xbuf
        par = i % 2
        s = sml[par]
        v.wait_ge(xld[par].h, XLD_AT[i])
        for j in range(4):
            v.bn_stats(st6[par][:, j * 6:(j + 1) * 6],
                       bufs[par][:, j * 512:(j + 1) * 512])
        v.drain()
        v.bn_aggr(ag[par][:], st6[par][:])
        v.drain()
        v.tensor_scalar(s[:, 0:1], ag[par][:, 1:2], EPS, None, AluOpType.add)
        v.drain()
        v.reciprocal(s[:, 1:2], s[:, 0:1])
        v.drain()
        vstat.tick(v.engine_nop())
        v.wait_ge(snorm.h, 2 * i + 1)
        v.tensor_tensor(s[:, 2:3], s[:, 7:8], s[:, 7:8], AluOpType.mult)
        v.drain()
        v.tensor_tensor(s[:, 3:4], s[:, 0:1], s[:, 2:3], AluOpType.mult)
        v.drain()
        v.tensor_scalar(s[:, 4:5], s[:, 3:4], -0.5, 1.5,
                        AluOpType.mult, AluOpType.add)
        v.drain()
        v.tensor_tensor(s[:, 5:6], s[:, 7:8], s[:, 4:5], AluOpType.mult)
        v.drain()
        v.tensor_scalar(s[:, 6:7], ag[par][:, 0:1], -1.0, None, AluOpType.mult)
        v.drain()
        v.tensor_tensor(s[:, 2:3], s[:, 6:7], s[:, 5:6], AluOpType.mult)
        v.drain()
        vstat.tick(v.engine_nop())

    def ln_scalar(sc, i, outbuf, guards=(), bufs=None):
        bufs = bufs or xbuf
        par = i % 2
        sc.wait_ge(vstat.h, ONES_AT + 2 * i + 1)
        snorm.tick(sc.activation(sml[par][:, 7:8], sml[par][:, 1:2], AF.Sqrt))
        sc.wait_ge(vstat.h, ONES_AT + 2 * i + 2)
        for gs, gv in guards:
            if gv > 0:
                sc.wait_ge(gs.h, gv)
        snorm.tick(sc.activation(outbuf[:], bufs[par][:], AF.Identity,
                                 scale=sml[par][:, 5:6], bias=sml[par][:, 2:3]))

    def ln_block(nt, base, src, row_off, norm_out, norm_guards, g_extra=None,
                 g_post=None):
        @blk.gpsimd
        def _(g):
            for t in range(nt):
                ln_gpsimd(g, base + t, src, row_off + t * 128,
                          g_extra if t == 0 else None)
            if g_post is not None:
                g_post(g)

        @blk.vector
        def _(v):
            for t in range(nt):
                ln_vec_ab(v, base + t)

        @blk.scalar
        def _(sc):
            for t in range(nt):
                i = base + t
                ln_scalar(sc, i, norm_out(i), norm_guards(i))

    def emit_quant(v, srcbuf, F, amB, s2B, ivB, q8B, dqB):
        nb = F // 32
        src3 = srcbuf[:, 0:F].rearrange("p (g k) -> p g k", k=32)
        v.drain()
        v.tensor_reduce(amB[:, 0:nb], src3, axis=AX.X, op=AluOpType.max,
                        apply_absolute_value=True)
        v.drain()
        v.tensor_scalar(s2B[:, 0:nb], amB[:, 0:nb], AMAX_FLOOR, INV224,
                        AluOpType.max, AluOpType.mult)
        v.drain()
        v.reciprocal(ivB[:, 0:nb], s2B[:, 0:nb])
        v.drain()
        v.tensor_tensor(q8B[:, 0:F].rearrange("p (g k) -> p g k", k=32), src3,
                        ivB[:, 0:nb].unsqueeze(2).broadcast_to([128, nb, 32]),
                        AluOpType.mult)
        v.drain()
        v.tensor_tensor(dqB[:, 0:F].rearrange("p (g k) -> p g k", k=32),
                        q8B[:, 0:F].rearrange("p (g k) -> p g k", k=32),
                        s2B[:, 0:nb].unsqueeze(2).broadcast_to([128, nb, 32]),
                        AluOpType.mult)
        v.drain()

    NPREP = 192

    def prep_src(j):
        if j < 128:
            It = j // 2
            w = gw if j % 2 == 0 else uw
            return w[It * 128:(It + 1) * 128, :]
        n = j - 128
        return dw[(n // 4) * 128:(n // 4 + 1) * 128,
                  (n % 4) * 2048:(n % 4 + 1) * 2048]

    def prep_dst(j):
        if j < 128:
            It = j // 2
            w = gT_s if j % 2 == 0 else uT_s
            return w[It * 128:(It + 1) * 128, :]
        n = j - 128
        return dT_s[(n // 4) * 128:(n // 4 + 1) * 128,
                    (n % 4) * 2048:(n % 4 + 1) * 2048]

    P = {"ld": 0, "vp": 0, "dt": 0, "st": 0, "vps": 0}

    def prep_region():
        P["vps"] = P["vp"]

    def prep_gp(g, nloads, nstores, plan=0, tight=False):
        k = 0
        while P["st"] < min(P["dt"], NPREP) and k < nstores:
            j = P["st"]
            g.wait_ge(tprep.h, 16 * (j + 1))
            wsst.pdma(g, prep_dst(j), gbuf[j % 2][:])
            P["st"] += 1
            k += 1
        k = 0
        if tight:
            cap = min(2 * P["vps"] + 2, max(P["dt"] + 2, 2), NPREP)
        else:
            cap = min(2 * (P["vp"] + plan) + 2, max(P["st"] + 4, 4), NPREP)
        while P["ld"] < cap and k < nloads:
            j = P["ld"]
            if j >= 2:
                if tight:
                    g.wait_ge(tprep.h, 16 * (j - 1))
                g.wait_ge(gq.h, j - 1)
            wgld[j % 2].dma(g, wwb[j % 2][:], prep_src(j))
            P["ld"] += 1
            k += 1

    def prep_vec(v, npairs):
        k = 0
        while P["vp"] < min(P["ld"] // 2, NPREP // 2) and k < npairs:
            pk = P["vp"]
            a, b = 2 * pk, 2 * pk + 1
            v.wait_ge(wgld[0].h, 16 * (pk + 1))
            v.wait_ge(wgld[1].h, 16 * (pk + 1))
            if pk >= 1:
                v.wait_ge(tprep.h, 16 * (2 * pk))
            v.drain()
            sa = wwb[0][:].rearrange("p (g k) -> p g k", k=32)
            sbv = wwb[1][:].rearrange("p (g k) -> p g k", k=32)
            v.tensor_reduce(amq2[:, 0:64], sa, axis=AX.X, op=AluOpType.max,
                            apply_absolute_value=True)
            v.tensor_reduce(amq2[:, 64:128], sbv, axis=AX.X, op=AluOpType.max,
                            apply_absolute_value=True)
            v.drain()
            v.tensor_scalar(s2q2[:], amq2[:], AMAX_FLOOR, INV224,
                            AluOpType.max, AluOpType.mult)
            v.drain()
            v.reciprocal(ivq2[:], s2q2[:])
            v.drain()
            v.tensor_tensor(q8a[:].rearrange("p (g k) -> p g k", k=32), sa,
                            ivq2[:, 0:64].unsqueeze(2).broadcast_to([128, 64, 32]),
                            AluOpType.mult)
            v.tensor_tensor(q8b[:].rearrange("p (g k) -> p g k", k=32), sbv,
                            ivq2[:, 64:128].unsqueeze(2).broadcast_to([128, 64, 32]),
                            AluOpType.mult)
            v.drain()
            v.tensor_tensor(gdqb[:].rearrange("p (g k) -> p g k", k=32),
                            q8a[:].rearrange("p (g k) -> p g k", k=32),
                            s2q2[:, 0:64].unsqueeze(2).broadcast_to([128, 64, 32]),
                            AluOpType.mult)
            v.tensor_tensor(udqb[:].rearrange("p (g k) -> p g k", k=32),
                            q8b[:].rearrange("p (g k) -> p g k", k=32),
                            s2q2[:, 64:128].unsqueeze(2).broadcast_to([128, 64, 32]),
                            AluOpType.mult)
            v.drain()
            gq.tick(v.engine_nop())
            gq.tick(v.engine_nop())
            P["vp"] += 1
            k += 1

    def prep_sync(sy, nd, nloads=2):
        k = 0
        while P["dt"] < min(2 * P["vp"], P["st"] + 2, NPREP) and k < nd:
            j = P["dt"]
            sy.wait_ge(gq.h, j + 1)
            if j >= 2:
                sy.wait_ge(wsst.h, 16 * (j - 1))
            src = gdqb if j % 2 == 0 else udqb
            dmat(sy, tprep, gbuf[j % 2][:].rearrange("p (c d) -> p c d", d=128),
                 src[:])
            P["dt"] += 1
            k += 1


    p1s = ExitStack()
    xnTh = sb(p1s, "xnTh", [128, HC, 1024], BF16)
    wkb = [sb(p1s, f"wkb{p}", [128, HC, 256], BF16) for p in range(2)]
    wvb = [sb(p1s, f"wvb{p}", [128, HC, 256], BF16) for p in range(2)]
    kevb = [sb(p1s, f"kevb{p}", [128, 512], BF16) for p in range(2)]
    vevb = [sb(p1s, f"vevb{p}", [128, 256], BF16) for p in range(2)]

    kst_cnt = [0]
    vst_cnt = [0]
    NDG = 8
    XNT_AT = {}

    def kv_wload(g, qq, dg):
        bg = qq * NDG + dg
        par = bg % 2
        if bg >= 2:
            g.wait_ge(kmm.h, (bg - 1) * 4)
            g.wait_ge(vmm.h, (bg - 1) * 8)
        kvwld[par].dma(g, wkb[par][:],
                       wk[:, dg * 256:(dg + 1) * 256]
                       .rearrange("(c p) d -> p c d", p=128))
        kvwld[par].dma(g, wvb[par][:],
                       wv[:, dg * 256:(dg + 1) * 256]
                       .rearrange("(c p) d -> p c d", p=128))

    for qq in range(4):
        base = qq * 8
        prep_region()

        @blk.gpsimd
        def _(g, qq=qq):
            for dg in range(2):
                kv_wload(g, qq, dg)
            prep_gp(g, 4, 0)

        ln_block(8, base, xb, qq * 1024,
                 norm_out=lambda i: xnbf[i % 2],
                 norm_guards=lambda i: [(tdone, 16 * (i - 1))])

        @blk.vector
        def _(v, qq=qq):
            prep_vec(v, 8)

        @blk.sync
        def _(sy, qq=qq):
            if qq >= 1:
                sy.wait_ge(kmm.h, 32 * qq)
                sy.wait_ge(vmm.h, 64 * qq)
            for t in range(8):
                i = qq * 8 + t
                sy.wait_ge(snorm.h, 2 * i + 2)
                XNT_AT[i] = dmat(sy, tdone,
                                 xnTh[:, :, t * 128:(t + 1) * 128],
                                 xnbf[i % 2][:])

        for dg in range(NDG):
            prep_region()

            @blk.gpsimd
            def _(g, qq=qq, dg=dg):
                for dtl in range(2):
                    dt = dg * 2 + dtl
                    for w in range(2):
                        kt = kst_cnt[0]
                        kst_cnt[0] += 1
                        g.wait_ge(kev.h, kt + 1)
                        kst[kt % 2].pdma(
                            g, kT_s[dt * 128:(dt + 1) * 128,
                                    qq * 1024 + w * 512:qq * 1024 + (w + 1) * 512],
                            kevb[kt % 2][:])
                for tt in range(8):
                    vt = vst_cnt[0]
                    vst_cnt[0] += 1
                    g.wait_ge(vev.h, vt + 1)
                    vst[vt % 2].pdma(
                        g, v_s[(qq * 8 + tt) * 128:(qq * 8 + tt + 1) * 128,
                               dg * 256:(dg + 1) * 256],
                        vevb[vt % 2][:])
                if dg + 2 < NDG:
                    kv_wload(g, qq, dg + 2)
                prep_gp(g, 4, 4, plan=2)

            @blk.vector
            def _(v, qq=qq, dg=dg):
                prep_vec(v, 2)

            @blk.scalar
            def _(sc, qq=qq, dg=dg):
                for j in range(4):
                    kt = kev.n
                    sc.wait_ge(kmm.h, kt + 1)
                    if kt >= 2:
                        sc.wait_ge(kst[kt % 2].h, 16 * (kt // 2))
                    kev.tick(sc.activation(kevb[kt % 2][:], ps[kt % 2][:],
                                           AF.Copy))
                for j in range(8):
                    vt = vev.n
                    sc.wait_ge(vmm.h, vt + 1)
                    if vt >= 2:
                        sc.wait_ge(vst[vt % 2].h, 16 * (vt // 2))
                    vev.tick(sc.activation(vevb[vt % 2][:],
                                           ps[2 + vt % 2][:, 0:256], AF.Copy))

            @blk.sync
            def _(sy):
                prep_sync(sy, 4)

            @blk.gpsimd
            def _(g, qq=qq, dg=dg):
                prep_gp(g, 2, 2)

            @blk.sync
            def _(sy):
                prep_sync(sy, 2)

        @blk.tensor
        def _(te, qq=qq):
            if qq == 0:
                te.wait_ge(cst.h, 64)
                te.wait_ge(vstat.h, ONES_AT)
            for dg in range(NDG):
                bg = qq * NDG + dg
                par = bg % 2
                te.wait_ge(kvwld[par].h, 32 * (bg // 2 + 1))
                for dtl in range(2):
                    dt = dg * 2 + dtl
                    for w in range(2):
                        kt = kmm.n
                        bank = ps[kt % 2]
                        if kt >= 2:
                            te.wait_ge(kev.h, kt - 1)
                        te.wait_ge(tdone.h, XNT_AT[qq * 8 + 4 * (w + 1) - 1])
                        for hc in range(HC):
                            te.matmul(bank[:],
                                      wkb[par][:, hc, dtl * 128:(dtl + 1) * 128],
                                      xnTh[:, hc, w * 512:(w + 1) * 512],
                                      start=(hc == 0), stop=False)
                        kmm.tick(te.matmul(bank[:],
                                           bias_sb[1][0:1, dt * 128:(dt + 1) * 128],
                                           ones_bf[0:1, :],
                                           start=False, stop=True))
                for tt in range(8):
                    vt = vmm.n
                    bank = ps[2 + vt % 2]
                    if vt >= 2:
                        te.wait_ge(vev.h, vt - 1)
                    te.wait_ge(tdone.h, XNT_AT[qq * 8 + tt])
                    for hc in range(HC):
                        te.matmul(bank[:, 0:256],
                                  xnTh[:, hc, tt * 128:(tt + 1) * 128],
                                  wvb[par][:, hc, :],
                                  start=(hc == 0), stop=False)
                    vmm.tick(te.matmul(bank[:, 0:256],
                                       ones_bf[0:1, 0:128],
                                       bias_sb[2][0:1, dg * 256:(dg + 1) * 256],
                                       start=False, stop=True))


    prep_region()

    @blk.gpsimd
    def _(g):
        g.wait_ge(kmm.h, 128)
        g.wait_ge(vmm.h, 256)
        for dg in range(2):
            qwld[dg % 2].dma(g, wkb[dg % 2][:],
                             wq[:, dg * 256:(dg + 1) * 256]
                             .rearrange("(c p) d -> p c d", p=128))

    ln_block(8, 32, xo, 0,
             norm_out=lambda i: xnbf[i % 2],
             norm_guards=lambda i: [(tdone, 16 * (i - 1))])

    @blk.gpsimd
    def _(g):
        for dg in range(2, NDG):
            par = dg % 2
            g.wait_ge(qmm.h, (dg - 1) * 4)
            qwld[par].dma(g, wkb[par][:],
                          wq[:, dg * 256:(dg + 1) * 256]
                          .rearrange("(c p) d -> p c d", p=128))
        prep_gp(g, 8, 8, plan=2)

    @blk.vector
    def _(v):
        prep_vec(v, 6)

    @blk.sync
    def _(sy):
        sy.wait_ge(kmm.h, 128)
        sy.wait_ge(vmm.h, 256)
        for t in range(8):
            i = 32 + t
            sy.wait_ge(snorm.h, 2 * i + 2)
            XNT_AT[i] = dmat(sy, tdone, xnTh[:, :, t * 128:(t + 1) * 128],
                             xnbf[i % 2][:])
        prep_sync(sy, 6)

    @blk.tensor
    def _(te):
        for dg in range(NDG):
            par = dg % 2
            te.wait_ge(qwld[par].h, 16 * (dg // 2 + 1))
            for dtl in range(2):
                dt = dg * 2 + dtl
                for w in range(2):
                    qt = qmm.n
                    bank = ps[4 + qt % 2]
                    if qt >= 2:
                        te.wait_ge(qev.h, qt - 1)
                    te.wait_ge(tdone.h, XNT_AT[32 + 4 * (w + 1) - 1])
                    for hc in range(HC):
                        te.matmul(bank[:],
                                  wkb[par][:, hc, dtl * 128:(dtl + 1) * 128],
                                  xnTh[:, hc, w * 512:(w + 1) * 512],
                                  start=(hc == 0), stop=False)
                    qmm.tick(te.matmul(bank[:],
                                       bias_sb[0][0:1, dt * 128:(dt + 1) * 128],
                                       ones_bf[0:1, :],
                                       start=False, stop=True))

    @blk.scalar
    def _(sc):
        qcnt = [0]
        for dg in range(NDG):
            for dtl in range(2):
                dt = dg * 2 + dtl
                for w in range(2):
                    qt = qcnt[0]
                    qcnt[0] += 1
                    sc.wait_ge(qmm.h, qt + 1)
                    qev.tick(sc.activation(QT[:, dt, w * 512:(w + 1) * 512],
                                           ps[4 + qt % 2][:], AF.Copy))

    p1s.close()
    xnbfS.close()

    p2s = ExitStack()
    krow = [[sb(p2s, f"krow{s_}_{i}", [128, S], BF16) for i in range(2)]
            for s_ in range(2)]
    vchunk = [sb(p2s, f"vchunk{p}", [128, 2 * 128], BF16) for p in range(2)]
    PT = [sb(p2s, f"PT{p}", [128, 1024], BF16) for p in range(2)]
    izv = sb(p2s, "izv", [1, 1024], F32)
    izb = sb(p2s, "izb", [128, 1024], F32)

    def krld_target(p):
        st = p % 2
        return 16 * sum(GLEN[r] for r in range(p + 1) if r % 2 == st)

    AVCUM = []
    for p, hs in enumerate(PASSES):
        for kc in range(32):
            AVCUM.append(2 * (PT0[p] + (kc + 1) * len(hs)))

    for p, hs in enumerate(PASSES):
        prep_region()

        for sub in range(8):
            prep_region()

            @blk.gpsimd
            def _(g, p=p, hs=hs, sub=sub):
                st = p % 2
                if p == 0 and sub == 0:
                    g.wait_ge(qev.h, 32)
                    for pp in range(2):
                        g.wait_ge(kst[pp].h, kst[pp].n)
                        g.wait_ge(vst[pp].h, vst[pp].n)
                if sub == 0:
                    if p >= 2:
                        g.wait_ge(smm.h, 2 * PT0[p - 1])
                    for i, h in enumerate(hs):
                        krld[st].dma(g, krow[st][i][:],
                                     kT_s[h * 128:(h + 1) * 128, :])
                for kc in range(sub * 4, sub * 4 + 4):
                    n = p * 32 + kc
                    if n >= 2:
                        g.wait_ge(ptav.h, AVCUM[n - 2])
                    d0 = hs[0] * 128
                    vcld[n % 2].dma(g, vchunk[n % 2][:, 0:len(hs) * 128],
                                    v_s[kc * 128:(kc + 1) * 128,
                                        d0:d0 + len(hs) * 128])
                prep_gp(g, 4, 4, plan=2)

            if sub % 2 == 1:
                @blk.vector
                def _(v, p=p, sub=sub):
                    prep_vec(v, 1)

            @blk.sync
            def _(sy, p=p, sub=sub):
                prep_sync(sy, 4)

        @blk.tensor
        def _(te, p=p, hs=hs):
            st = p % 2
            gl = len(hs)
            if p == 0:
                te.wait_ge(qev.h, 32)
            te.wait_ge(krld[st].h, krld_target(p))
            te.wait_ge(oev.h, 2 * GH0[p])
            for kc in range(32):
                n = p * 32 + kc
                te.wait_ge(vcld[n % 2].h, 16 * (n // 2 + 1))
                for i in range(gl):
                    ptn = PT0[p] + kc * gl + i
                    for qh in range(2):
                        scn = 2 * ptn + qh
                        if scn >= 2:
                            te.wait_ge(sexp.h, scn - 1)
                        smm.tick(te.matmul(ps[qh][:],
                                           krow[st][i][:, kc * 128:(kc + 1) * 128],
                                           QT[:, hs[i], qh * 512:(qh + 1) * 512],
                                           start=True, stop=True))
                for i in range(gl):
                    ptn = PT0[p] + kc * gl + i
                    for qh in range(2):
                        te.wait_ge(sexp.h, 2 * ptn + qh + 1)
                        te.matmul(ps[2 + 2 * i + qh][:],
                                  vchunk[n % 2][:, i * 128:(i + 1) * 128],
                                  PT[ptn % 2][:, qh * 512:(qh + 1) * 512],
                                  start=(kc == 0), stop=(kc == 31))
                        ptav.tick(te.matmul(
                            ps[6 + qh][32 * i:32 * i + 1, :], ones_cbf[:],
                            PT[ptn % 2][:, qh * 512:(qh + 1) * 512],
                            start=(kc == 0), stop=(kc == 31)))
            te.wait_ge(sexp.h, 2 * PT0[p + 1])
            for i in range(gl):
                gh = GH0[p] + i
                if i > 0:
                    te.wait_ge(oev.h, 2 * gh)
                te.wait_ge(vzf.h, gh + 1)
                for qh in range(2):
                    bcmm.tick(te.matmul(ps[qh][:], ones_row[0:1, 0:128],
                                        izv[0:1, qh * 512:(qh + 1) * 512],
                                        start=True, stop=True))

        @blk.scalar
        def _(sc, p=p, hs=hs):
            gl = len(hs)
            for kc in range(32):
                for i in range(gl):
                    ptn = PT0[p] + kc * gl + i
                    for qh in range(2):
                        scn = 2 * ptn + qh
                        sc.wait_ge(smm.h, scn + 1)
                        if ptn >= 2:
                            sc.wait_ge(ptav.h, 2 * (ptn - 1))
                        sexp.tick(sc.activation(
                            PT[ptn % 2][:, qh * 512:(qh + 1) * 512],
                            ps[qh][:], AF.Exp, scale=SCL))

        @blk.vector
        def _(v, p=p, hs=hs):
            gl = len(hs)
            for i in range(gl):
                gh = GH0[p] + i
                v.wait_ge(bcmm.h, 2 * gh)
                v.wait_ge(ptav.h, 2 * (PT0[p] + 31 * gl + i + 1))
                v.reciprocal(izv[0:1, 0:512], ps[6][32 * i:32 * i + 1, :])
                v.reciprocal(izv[0:1, 512:1024], ps[7][32 * i:32 * i + 1, :])
                vzf.tick(v.drain())
                v.wait_ge(bcmm.h, 2 * gh + 2)
                v.tensor_copy(izb[:, 0:512], ps[0][:])
                v.tensor_copy(izb[:, 512:1024], ps[1][:])
                v.drain()
                for qh in range(2):
                    v.tensor_tensor(AVT[:, hs[i], qh * 512:(qh + 1) * 512],
                                    ps[2 + 2 * i + qh][:],
                                    izb[:, qh * 512:(qh + 1) * 512],
                                    AluOpType.mult)
                    oev.tick(v.drain())

        @blk.vector
        def _(v, p=p):
            prep_vec(v, 2)

    p2s.close()
    QTs.close()

    p3s = ExitStack()
    woblk = [sb(p3s, f"woblk{og}", [128, HC, 512], BF16) for og in range(4)]
    outb = [sb(p3s, f"outb{p}", [128, H], F32) for p in range(2)]
    bo_sb = sb(p3s, "bo_sb", [1, H], BF16)
    bost = sem("bost")

    XLD3 = []
    _x0, _x1 = xld[0].n, xld[1].n
    for tt in range(8):
        if tt % 2 == 0:
            _x0 += 16
            XLD3.append(_x0)
        else:
            _x1 += 16
            XLD3.append(_x1)

    for tt in range(8):
        prep_region()

        @blk.gpsimd
        def _(g, tt=tt):
            par = tt % 2
            if tt == 0:
                g.wait_ge(oev.h, 32)
                bost.dma(g, bo_sb[:], bias4[3:4, :])
                for og in range(4):
                    wowld.dma(g, woblk[og][:],
                              wo[:, og * 512:(og + 1) * 512]
                              .rearrange("(c p) d -> p c d", p=128))
                g.wait_ge(snorm.h, 80)
            if tt >= 2:
                g.wait_ge(p3ev.h, 4 * (tt - 1))
            xld[par].dma(g, xbuf[par][:], xo[tt * 128:(tt + 1) * 128, :])
            if tt >= 2:
                g.wait_ge(p3ev.h, 4 * (tt - 1))
                ost[tt % 2].pdma(g, out[(tt - 2) * 128:(tt - 1) * 128, :],
                                 outb[tt % 2][:])
            prep_gp(g, 4, 4, plan=2)

        @blk.vector
        def _(v, tt=tt):
            par = tt % 2
            v.wait_ge(xld[par].h, XLD3[tt])
            if tt >= 2:
                v.wait_ge(ost[par].h, 16 * (tt // 2))
            for og in range(4):
                nn = tt * 4 + og
                v.wait_ge(p3mm.h, nn + 1)
                v.drain()
                v.tensor_tensor(outb[par][:, og * 512:(og + 1) * 512],
                                ps[nn % 2][:],
                                xbuf[par][:, og * 512:(og + 1) * 512],
                                AluOpType.add)
                p3ev.tick(v.drain())
            prep_vec(v, 1)

        @blk.gpsimd
        def _(g, tt=tt):
            prep_gp(g, 2, 2, plan=1)

        @blk.vector
        def _(v, tt=tt):
            prep_vec(v, 1)

        @blk.sync
        def _(sy, tt=tt):
            prep_sync(sy, 6, 4)

    @blk.gpsimd
    def _(g):
        for tt in range(6, 8):
            g.wait_ge(p3ev.h, 4 * (tt + 1))
            ost[tt % 2].pdma(g, out[tt * 128:(tt + 1) * 128, :], outb[tt % 2][:])

    @blk.tensor
    def _(te):
        te.wait_ge(oev.h, 32)
        te.wait_ge(wowld.h, 64)
        te.wait_ge(bost.h, 16)
        for tt in range(8):
            for og in range(4):
                nn = tt * 4 + og
                bank = ps[nn % 2]
                if nn >= 2:
                    te.wait_ge(p3ev.h, nn - 1)
                for hc in range(HC):
                    te.matmul(bank[:],
                              AVT[:, hc, tt * 128:(tt + 1) * 128],
                              woblk[og][:, hc, :],
                              start=(hc == 0), stop=False)
                p3mm.tick(te.matmul(bank[:],
                                    ones_bf[0:1, 0:128],
                                    bo_sb[0:1, og * 512:(og + 1) * 512],
                                    start=False, stop=True))

    @blk.sync
    def _(sy):
        prep_sync(sy, 999)


    p3s.close()
    AVTs.close()
    xbufS.close()
    biasS.close()

    p4s = ExitStack()
    xdqT = sb(p4s, "xdqT", [128, HC, TO], BF16)
    xwb = [sb(p4s, f"xwb{p}", [128, H], F32) for p in range(2)]
    xdq = [sb(p4s, f"xdq{p}", [128, H], BF16) for p in range(2)]
    gbufL = [sb(p4s, f"gbufL{p}", [128, H], BF16) for p in range(2)]
    ubufL = [sb(p4s, f"ubufL{p}", [128, H], BF16) for p in range(2)]
    sgb = [sb(p4s, f"sgb{p}", [128, 512], F32) for p in range(2)]
    hfb = sb(p4s, "hfb", [128, 512], F32)
    htt = sb(p4s, "htt", [128, 512], F32)
    amh = sb(p4s, "amh", [128, 16], F32)
    s2h = sb(p4s, "s2h", [128, 16], F32)
    ivh = sb(p4s, "ivh", [128, 16], F32)
    h8b = sb(p4s, "h8b", [128, 512], FP8)
    hdt = sb(p4s, "hdt", [128, 512], BF16)
    hdq = sb(p4s, "hdq", [128, 64 * 512], BF16)
    dbufp = [sb(p4s, f"dbufp{p}", [128, H], BF16) for p in range(2)]
    ftb = [sb(p4s, f"ftb{p}", [128, 512], F32) for p in range(2)]
    tbuf = [sb(p4s, f"tbuf{p}", [128, 128], F32) for p in range(2)]

    TD_P4A = tdone.n

    @blk.gpsimd
    def _(g):
        for t in range(8):
            i = 40 + t
            par = i % 2
            if t == 0:
                g.wait_ge(p3ev.h, 32)
                g.wait_ge(p3mm.h, 32)
                for pp in range(2):
                    g.wait_ge(ost[pp].h, ost[pp].n)
            if t >= 2:
                g.wait_ge(xq.h, t - 1)
            XLD_AT[i] = xld[par].dma(g, xwb[par][:], xo[t * 128:(t + 1) * 128, :])
            prep_gp(g, 2, 2)

    def p4a_quant(v, t):
        par = (40 + t) % 2
        v.wait_ge(snorm.h, 2 * (40 + t) + 2)
        if t >= 2:
            v.wait_ge(tdone.h, TD_P4A + 16 * (t - 1))
        emit_quant(v, xwb[par], H, amq2, s2q2, ivq2, q8a, xdq[par])
        xq.tick(v.engine_nop())

    @blk.vector
    def _(v):
        for t in range(8):
            prep_region()
            ln_vec_ab(v, 40 + t, bufs=xwb)
            if t >= 1:
                p4a_quant(v, t - 1)
        p4a_quant(v, 7)

    @blk.scalar
    def _(sc):
        for t in range(8):
            i = 40 + t
            ln_scalar(sc, i, xwb[i % 2], bufs=xwb)

    @blk.sync
    def _(sy):
        for t in range(8):
            sy.wait_ge(xq.h, t + 1)
            dmat(sy, tdone, xdqT[:, :, t * 128:(t + 1) * 128], xdq[t % 2][:])
            prep_sync(sy, 3)

    NCH = 4

    def emit_hpipe(v, th, It):
        gg = th * 64 + It
        v.wait_ge(hsil.h, gg + 1)
        v.drain()
        v.tensor_tensor(hfb[:], sgb[gg % 2][:], ps[2 + gg % 2][:],
                        AluOpType.mult)
        v.drain()
        v.transpose(htt[:], hfb[:])
        v.drain()
        ht3 = htt[:].rearrange("p (g k) -> p g k", k=32)
        v.tensor_reduce(amh[:], ht3, axis=AX.X, op=AluOpType.max,
                        apply_absolute_value=True)
        v.drain()
        v.tensor_scalar(s2h[:], amh[:], AMAX_FLOOR, INV224,
                        AluOpType.max, AluOpType.mult)
        v.drain()
        v.reciprocal(ivh[:], s2h[:])
        v.drain()
        v.tensor_tensor(h8b[:].rearrange("p (g k) -> p g k", k=32), ht3,
                        ivh[:].unsqueeze(2).broadcast_to([128, 16, 32]),
                        AluOpType.mult)
        v.drain()
        v.tensor_tensor(hdt[:].rearrange("p (g k) -> p g k", k=32),
                        h8b[:, 0:512].rearrange("p (g k) -> p g k", k=32),
                        s2h[:].unsqueeze(2).broadcast_to([128, 16, 32]),
                        AluOpType.mult)
        v.drain()
        v.transpose(hdq[:, It * 512:(It + 1) * 512], hdt[:])
        hq.tick(v.drain())

    def emit_tcopy(v, th, gg):
        for tq in range(4):
            m = 4 * gg + tq
            v.wait_ge(ttr.h, 4 * gg + 4)
            if m >= 2:
                v.wait_ge(accst[m % 2].h, 16 * (m // 2))
            v.tensor_copy(tbuf[m % 2][:],
                          ps[6 + gg % 2][:, tq * 128:(tq + 1) * 128])
            tbe.tick(v.drain())

    def emit_accums(g, gg):
        for tq in range(4):
            m = 4 * gg + tq
            g.wait_ge(tbe.h, m + 1)
            th_, ho_ = gg // 16, gg % 16
            accst[m % 2].pdma(
                g, out[(th_ * 4 + tq) * 128:(th_ * 4 + tq + 1) * 128,
                       ho_ * 128:(ho_ + 1) * 128],
                tbuf[m % 2][:], accum_op=AluOpType.add)

    for th in range(2):
        for It in range(64):
            prep_region()

            @blk.gpsimd
            def _(g, th=th, It=It):
                par = It % 2
                gg = th * 64 + It
                prep_gp(g, 2, 2, tight=True)
                assert P["st"] >= 2 * It + 2, (P, th, It)
                if gg >= 2:
                    g.wait_ge(p4mm.h, gg - 1)
                g.wait_ge(wsst.h, 16 * (2 * It + 2))
                gsld[par].dma(g, gbufL[par][:], gT_s[It * 128:(It + 1) * 128, :])
                gsld[par].dma(g, ubufL[par][:], uT_s[It * 128:(It + 1) * 128, :])

            @blk.vector
            def _(v, th=th, It=It):
                if th == 1 and It == 0:
                    v.wait_ge(p4dpc.h, 64)
                emit_hpipe(v, th, It)
                prep_vec(v, 1)

            @blk.sync
            def _(sy, th=th, It=It):
                prep_sync(sy, 2)

        @blk.tensor
        def _(te, th=th):
            if th == 0:
                te.wait_ge(p3ev.h, 32)
                te.wait_ge(tdone.h, TD_P4A + 16 * 8)
            for It in range(64):
                par = It % 2
                gg = th * 64 + It
                gb = gbufL[par][:].rearrange("p (c d) -> p c d", d=128)
                ub = ubufL[par][:].rearrange("p (c d) -> p c d", d=128)
                te.wait_ge(gsld[par].h, 32 * (th * 32 + It // 2 + 1))
                if gg >= 2:
                    te.wait_ge(hsil.h, gg - 1)
                    te.wait_ge(hq.h, gg - 1)
                for hc in range(HC):
                    te.matmul(ps[gg % 2][:], gb[:, hc, :],
                              xdqT[:, hc, th * 512:(th + 1) * 512],
                              start=(hc == 0), stop=(hc == HC - 1))
                for hc in range(HC):
                    ii = te.matmul(ps[2 + gg % 2][:], ub[:, hc, :],
                                   xdqT[:, hc, th * 512:(th + 1) * 512],
                                   start=(hc == 0), stop=(hc == HC - 1))
                p4mm.tick(ii)

        @blk.scalar
        def _(sc, th=th):
            for It in range(64):
                gg = th * 64 + It
                sc.wait_ge(p4mm.h, gg + 1)
                if gg >= 2:
                    sc.wait_ge(hq.h, gg - 1)
                hsil.tick(sc.activation(sgb[gg % 2][:], ps[gg % 2][:], AF.Silu))

        for ho in range(16):
            prep_region()

            @blk.gpsimd
            def _(g, th=th, ho=ho):
                prep_gp(g, 4, 4, tight=True)
                for ch in range(NCH):
                    n = ho * NCH + ch
                    gp = th * 64 + n
                    if gp >= 2:
                        g.wait_ge(p4dpc.h, gp - 1)
                    if th == 0:
                        assert P["st"] >= 129 + n, (P, th, ho, ch, n)
                        g.wait_ge(wsst.h, 16 * (129 + n))
                    dsld[gp % 2].dma(g, dbufp[gp % 2][:],
                                     dT_s[ho * 128:(ho + 1) * 128,
                                          ch * 2048:(ch + 1) * 2048])
                if ho >= 2:
                    emit_accums(g, th * 16 + ho - 2)
                if ho == 15:
                    for gg in range(th * 16 + 14, th * 16 + 16):
                        emit_accums(g, gg)

            @blk.vector
            def _(v, th=th, ho=ho):
                emit_tcopy(v, th, th * 16 + ho)
                prep_vec(v, 1)

            @blk.sync
            def _(sy, th=th, ho=ho):
                prep_sync(sy, 4, 4)

            @blk.gpsimd
            def _(g, th=th, ho=ho):
                prep_gp(g, 2, 2, tight=True)

            @blk.vector
            def _(v, th=th, ho=ho):
                prep_vec(v, 1)

            @blk.sync
            def _(sy, th=th, ho=ho):
                prep_sync(sy, 4, 4)

        @blk.tensor
        def _(te, th=th):
            te.wait_ge(hq.h, 64 * (th + 1))

            def down_mm(ho):
                gg = th * 16 + ho
                if gg >= 2:
                    te.wait_ge(fts.h, gg - 1)
                for ch in range(NCH):
                    n = ho * NCH + ch
                    gp = th * 64 + n
                    te.wait_ge(dsld[gp % 2].h, 16 * (gp // 2 + 1))
                    for icl in range(16):
                        ic = ch * 16 + icl
                        ii = te.matmul(ps[4 + gg % 2][:],
                                       dbufp[gp % 2][:, icl * 128:(icl + 1) * 128],
                                       hdq[:, ic * 512:(ic + 1) * 512],
                                       start=(ic == 0), stop=(ic == 63))
                    p4dpc.tick(ii)

            def transp(ho):
                gg = th * 16 + ho
                te.wait_ge(fts.h, gg + 1)
                if gg >= 2:
                    te.wait_ge(tbe.h, 4 * (gg - 1))
                for tq in range(4):
                    ttr.tick(te.transpose(
                        ps[6 + gg % 2][:, tq * 128:(tq + 1) * 128],
                        ftb[gg % 2][:, tq * 128:(tq + 1) * 128],
                        ident_sb[:]))

            for ho in range(16):
                down_mm(ho)
                if ho >= 2:
                    transp(ho - 2)
            transp(14)
            transp(15)

        @blk.scalar
        def _(sc, th=th):
            for ho in range(16):
                gg = th * 16 + ho
                sc.wait_ge(p4dpc.h, 4 * gg + 4)
                if gg >= 2:
                    sc.wait_ge(ttr.h, 4 * (gg - 1))
                fts.tick(sc.activation(ftb[gg % 2][:], ps[4 + gg % 2][:], AF.Copy))

    assert P["ld"] == NPREP and P["vp"] * 2 == NPREP
    assert P["dt"] == NPREP and P["st"] == NPREP

    @blk.gpsimd
    def _(g):
        for pp in range(2):
            g.wait_ge(accst[pp].h, accst[pp].n)
            g.wait_ge(ost[pp].h, ost[pp].n)

    p4s.close()
    prepS.close()
    es.close()
    return nc


def kernel(L0_attn_hidden_state, L0_attn_Wq, L0_attn_bq, L0_attn_Wk, L0_attn_bk,
           L0_attn_Wv, L0_attn_bv, L0_attn_Wo, L0_attn_bo,
           L0_ffn_gate_w, L0_ffn_up_w, L0_ffn_down_w):
    x = np.ascontiguousarray(np.asarray(L0_attn_hidden_state, dtype=np.float32))
    Wq = np.ascontiguousarray(np.asarray(L0_attn_Wq, dtype=np.float32))
    Wk = np.ascontiguousarray(np.asarray(L0_attn_Wk, dtype=np.float32))
    Wv = np.ascontiguousarray(np.asarray(L0_attn_Wv, dtype=np.float32))
    Wo = np.ascontiguousarray(np.asarray(L0_attn_Wo, dtype=np.float32))
    bias = np.ascontiguousarray(np.stack([
        np.asarray(L0_attn_bq, dtype=np.float32),
        np.asarray(L0_attn_bk, dtype=np.float32),
        np.asarray(L0_attn_bv, dtype=np.float32),
        np.asarray(L0_attn_bo, dtype=np.float32)]))
    Gw = np.ascontiguousarray(np.asarray(L0_ffn_gate_w, dtype=np.float32))
    Uw = np.ascontiguousarray(np.asarray(L0_ffn_up_w, dtype=np.float32))
    Dw = np.ascontiguousarray(np.asarray(L0_ffn_down_w, dtype=np.float32))
    ident = np.eye(128, dtype=np.float32)

    if "nc" not in _CACHE:
        _CACHE["nc"] = build_program()
    nc = _CACHE["nc"]

    in_maps = []
    for c in range(8):
        b, sl = c // 4, c % 4
        in_maps.append({
            "xb": x[b],
            "xo": np.ascontiguousarray(x[b, sl * TO:(sl + 1) * TO]),
            "wq": Wq, "wk": Wk, "wv": Wv, "wo": Wo,
            "bias4": bias, "gw": Gw, "uw": Uw, "dw": Dw,
            "ident": ident,
        })
    res = run_bass_kernel_spmd(nc, in_maps, core_ids=list(range(8)))
    outp = np.empty((2, S, H), dtype=np.float32)
    for c in range(8):
        b, sl = c // 4, c % 4
        outp[b, sl * TO:(sl + 1) * TO] = np.asarray(res.results[c]["out"])
    return outp


# version 12
# speedup vs baseline: 1.0089x; 1.0089x over previous
import numpy as np
from contextlib import ExitStack

import concourse.bass as bass
import concourse.mybir as mybir
from concourse.alu_op_type import AluOpType
from concourse.bass_utils import run_bass_kernel_spmd

F32 = mybir.dt.float32
BF16 = mybir.dt.bfloat16
FP8 = mybir.dt.float8e4
AF = mybir.ActivationFunctionType
AX = mybir.AxisListType

S = 4096
TO = 1024
H = 2048
II = 8192
HC = 16
EPS = float(np.float32(1e-5))
SCL = float(np.float32(1.0) / np.float32(np.sqrt(np.float32(128.0))))
AMAX_FLOOR = float(np.float32(448e-30))
INV224 = float(np.float32(1.0) / np.float32(224.0))

PASSES = [[2 * i, 2 * i + 1] for i in range(8)]
GLEN = [len(p) for p in PASSES]
PT0 = [sum(g * 32 for g in GLEN[:p]) for p in range(9)]
GH0 = [sum(GLEN[:p]) for p in range(9)]

_CACHE = {}


class Sem:
    __slots__ = ("h", "n")

    def __init__(self, h):
        self.h = h
        self.n = 0

    def dma(self, eng, out, in_, **kw):
        eng.dma_start(out=out, in_=in_, **kw).then_inc(self.h, 16)
        self.n += 16
        return self.n

    def pdma(self, eng, out, in_, **kw):
        if self.n:
            eng.wait_ge(self.h, self.n)
        return self.dma(eng, out, in_, **kw)

    def tick(self, instr):
        instr.then_inc(self.h, 1)
        self.n += 1
        return self.n


def build_program():
    nc = bass.Bass("TRN2", target_bir_lowering=False, debug=False)
    es = ExitStack()

    def sem(name):
        return Sem(es.enter_context(nc.semaphore(name)))

    def sb(stack, name, shape, dt):
        return stack.enter_context(nc.sbuf_tensor(name, shape, dt))

    xb = nc.dram_tensor("xb", [S, H], F32, kind="ExternalInput").ap()
    xo = nc.dram_tensor("xo", [TO, H], F32, kind="ExternalInput").ap()
    wq = nc.dram_tensor("wq", [H, H], F32, kind="ExternalInput").ap()
    wk = nc.dram_tensor("wk", [H, H], F32, kind="ExternalInput").ap()
    wv = nc.dram_tensor("wv", [H, H], F32, kind="ExternalInput").ap()
    wo = nc.dram_tensor("wo", [H, H], F32, kind="ExternalInput").ap()
    bias4 = nc.dram_tensor("bias4", [4, H], F32, kind="ExternalInput").ap()
    gw = nc.dram_tensor("gw", [II, H], F32, kind="ExternalInput").ap()
    uw = nc.dram_tensor("uw", [II, H], F32, kind="ExternalInput").ap()
    dw = nc.dram_tensor("dw", [H, II], F32, kind="ExternalInput").ap()
    ident = nc.dram_tensor("ident", [128, 128], F32, kind="ExternalInput").ap()
    out = nc.dram_tensor("out", [TO, H], F32, kind="ExternalOutput").ap()

    kT_s = nc.dram_tensor("kT_s", [H, S], BF16, kind="Internal").ap()
    v_s = nc.dram_tensor("v_s", [S, H], BF16, kind="Internal").ap()
    gT_s = nc.dram_tensor("gT_s", [II, H], BF16, kind="Internal").ap()
    uT_s = nc.dram_tensor("uT_s", [II, H], BF16, kind="Internal").ap()
    dT_s = nc.dram_tensor("dT_s", [H, II], BF16, kind="Internal").ap()

    ps = [es.enter_context(nc.psum_tensor(f"ps{i}", [128, 512], F32))
          for i in range(8)]

    ident_sb = sb(es, "ident_sb", [128, 128], F32)
    ones_row = sb(es, "ones_row", [1, 128], F32)
    ones_bf = sb(es, "ones_bf", [1, 512], BF16)
    ones_col = sb(es, "ones_col", [128, 1], F32)
    ones_cbf = sb(es, "ones_cbf", [128, 1], BF16)
    st6 = [sb(es, f"st6_{p}", [128, 24], F32) for p in range(2)]
    ag = [sb(es, f"ag{p}", [128, 2], F32) for p in range(2)]
    sml = [sb(es, f"sml{p}", [128, 8], F32) for p in range(2)]
    q8a = sb(es, "q8a", [128, H], FP8)
    q8b = sb(es, "q8b", [128, H], FP8)
    amq2 = sb(es, "amq2", [128, 128], F32)
    s2q2 = sb(es, "s2q2", [128, 128], F32)
    ivq2 = sb(es, "ivq2", [128, 128], F32)

    prepS = ExitStack()
    wwb = [sb(prepS, f"wwb{p}", [128, H], F32) for p in range(2)]
    gdqb = sb(prepS, "gdqb", [128, H], BF16)
    udqb = sb(prepS, "udqb", [128, H], BF16)
    gbuf = [sb(prepS, f"gbuf{p}", [128, H], BF16) for p in range(2)]

    biasS = ExitStack()
    bias_sb = [sb(biasS, f"bias_sb{i}", [1, H], BF16) for i in range(3)]
    xbufS = ExitStack()
    xbuf = [sb(xbufS, f"xbuf{p}", [128, H], F32) for p in range(2)]
    AVTs = ExitStack()
    AVT = sb(AVTs, "AVT", [128, HC, TO], BF16)
    QTs = ExitStack()
    QT = sb(QTs, "QT", [128, HC, TO], BF16)
    xnbfS = ExitStack()
    xnbf = [sb(xnbfS, f"xnbf{p}", [128, H], BF16) for p in range(2)]

    cst = sem("cst")
    xld = [sem(f"xld{p}") for p in range(2)]
    vstat = sem("vstat")
    snorm = sem("snorm")
    tdone = sem("tdone")
    tprep = sem("tprep")
    kvwld = [sem(f"kvwld{p}") for p in range(2)]
    qwld = [sem(f"qwld{p}") for p in range(2)]
    kmm = sem("kmm")
    vmm = sem("vmm")
    qmm = sem("qmm")
    kev = sem("kev")
    vev = sem("vev")
    qev = sem("qev")
    kst = [sem(f"kst{p}") for p in range(2)]
    vst = [sem(f"vst{p}") for p in range(2)]
    krld = [sem(f"krld{p}") for p in range(2)]
    vcld = [sem(f"vcld{p}") for p in range(2)]
    smm = sem("smm")
    sexp = sem("sexp")
    ptav = sem("ptav")
    vzf = sem("vzf")
    bcmm = sem("bcmm")
    oev = sem("oev")
    wowld = sem("wowld")
    p3mm = sem("p3mm")
    p3ev = sem("p3ev")
    ost = [sem(f"ost{p}") for p in range(2)]
    xq = sem("xq")
    wgld = [sem(f"wgld{p}") for p in range(2)]
    gq = sem("gq")
    wsst = sem("wsst")
    gsld = [sem(f"gsld{p}") for p in range(2)]
    p4mm = sem("p4mm")
    hsil = sem("hsil")
    hq = sem("hq")
    dsld = [sem(f"dsld{p}") for p in range(2)]
    p4dpc = sem("p4dpc")
    fts = sem("fts")
    ttr = sem("ttr")
    tbe = sem("tbe")
    accst = [sem(f"accst{p}") for p in range(2)]
    hb = sem("hb")

    blk = es.enter_context(nc.Block())

    def dmat(sy, dsem, out_ap, in_ap):
        if tdone.n:
            sy.wait_ge(tdone.h, tdone.n)
        if tprep.n:
            sy.wait_ge(tprep.h, tprep.n)
        sy.dma_start_transpose(out_ap, in_ap).then_inc(dsem.h, 16)
        dsem.n += 16
        return dsem.n

    @blk.gpsimd
    def _(g):
        for i in range(3):
            cst.dma(g, bias_sb[i][:], bias4[i:i + 1, :])
        cst.dma(g, ident_sb[:], ident)

    @blk.vector
    def _(v):
        v.memset(ones_row[:], 1.0)
        v.memset(ones_bf[:], 1.0)
        v.memset(ones_col[:], 1.0)
        v.memset(ones_cbf[:], 1.0)
        v.drain()
        vstat.tick(v.engine_nop())
    ONES_AT = vstat.n

    XLD_AT = {}

    def ln_gpsimd(g, i, src, row0, extra=None, bufs=None):
        bufs = bufs or xbuf
        par = i % 2
        if extra is not None:
            extra(g)
        if i >= 2:
            g.wait_ge(snorm.h, 2 * (i - 2) + 2)
        XLD_AT[i] = xld[par].dma(g, bufs[par][:], src[row0:row0 + 128, :])

    def ln_vec_ab(v, i, bufs=None):
        bufs = bufs or xbuf
        par = i % 2
        s = sml[par]
        v.wait_ge(xld[par].h, XLD_AT[i])
        for j in range(4):
            v.bn_stats(st6[par][:, j * 6:(j + 1) * 6],
                       bufs[par][:, j * 512:(j + 1) * 512])
        v.drain()
        v.bn_aggr(ag[par][:], st6[par][:])
        v.drain()
        v.tensor_scalar(s[:, 0:1], ag[par][:, 1:2], EPS, None, AluOpType.add)
        v.drain()
        v.reciprocal(s[:, 1:2], s[:, 0:1])
        v.drain()
        vstat.tick(v.engine_nop())
        v.wait_ge(snorm.h, 2 * i + 1)
        v.tensor_tensor(s[:, 2:3], s[:, 7:8], s[:, 7:8], AluOpType.mult)
        v.drain()
        v.tensor_tensor(s[:, 3:4], s[:, 0:1], s[:, 2:3], AluOpType.mult)
        v.drain()
        v.tensor_scalar(s[:, 4:5], s[:, 3:4], -0.5, 1.5,
                        AluOpType.mult, AluOpType.add)
        v.drain()
        v.tensor_tensor(s[:, 5:6], s[:, 7:8], s[:, 4:5], AluOpType.mult)
        v.drain()
        v.tensor_scalar(s[:, 6:7], ag[par][:, 0:1], -1.0, None, AluOpType.mult)
        v.drain()
        v.tensor_tensor(s[:, 2:3], s[:, 6:7], s[:, 5:6], AluOpType.mult)
        v.drain()
        vstat.tick(v.engine_nop())

    def ln_scalar(sc, i, outbuf, guards=(), bufs=None):
        bufs = bufs or xbuf
        par = i % 2
        sc.wait_ge(vstat.h, ONES_AT + 2 * i + 1)
        snorm.tick(sc.activation(sml[par][:, 7:8], sml[par][:, 1:2], AF.Sqrt))
        sc.wait_ge(vstat.h, ONES_AT + 2 * i + 2)
        for gs, gv in guards:
            if gv > 0:
                sc.wait_ge(gs.h, gv)
        snorm.tick(sc.activation(outbuf[:], bufs[par][:], AF.Identity,
                                 scale=sml[par][:, 5:6], bias=sml[par][:, 2:3]))

    def ln_block(nt, base, src, row_off, norm_out, norm_guards, g_extra=None,
                 g_post=None):
        @blk.gpsimd
        def _(g):
            for t in range(nt):
                ln_gpsimd(g, base + t, src, row_off + t * 128,
                          g_extra if t == 0 else None)
            if g_post is not None:
                g_post(g)

        @blk.vector
        def _(v):
            for t in range(nt):
                ln_vec_ab(v, base + t)

        @blk.scalar
        def _(sc):
            for t in range(nt):
                i = base + t
                ln_scalar(sc, i, norm_out(i), norm_guards(i))

    def emit_quant(v, srcbuf, F, amB, s2B, ivB, q8B, dqB):
        nb = F // 32
        src3 = srcbuf[:, 0:F].rearrange("p (g k) -> p g k", k=32)
        v.drain()
        v.tensor_reduce(amB[:, 0:nb], src3, axis=AX.X, op=AluOpType.max,
                        apply_absolute_value=True)
        v.drain()
        v.tensor_scalar(s2B[:, 0:nb], amB[:, 0:nb], AMAX_FLOOR, INV224,
                        AluOpType.max, AluOpType.mult)
        v.drain()
        v.reciprocal(ivB[:, 0:nb], s2B[:, 0:nb])
        v.drain()
        v.tensor_tensor(q8B[:, 0:F].rearrange("p (g k) -> p g k", k=32), src3,
                        ivB[:, 0:nb].unsqueeze(2).broadcast_to([128, nb, 32]),
                        AluOpType.mult)
        v.drain()
        v.tensor_tensor(dqB[:, 0:F].rearrange("p (g k) -> p g k", k=32),
                        q8B[:, 0:F].rearrange("p (g k) -> p g k", k=32),
                        s2B[:, 0:nb].unsqueeze(2).broadcast_to([128, nb, 32]),
                        AluOpType.mult)
        v.drain()

    NPREP = 192

    def prep_src(j):
        if j < 128:
            It = j // 2
            w = gw if j % 2 == 0 else uw
            return w[It * 128:(It + 1) * 128, :]
        n = j - 128
        return dw[(n // 4) * 128:(n // 4 + 1) * 128,
                  (n % 4) * 2048:(n % 4 + 1) * 2048]

    def prep_dst(j):
        if j < 128:
            It = j // 2
            w = gT_s if j % 2 == 0 else uT_s
            return w[It * 128:(It + 1) * 128, :]
        n = j - 128
        return dT_s[(n // 4) * 128:(n // 4 + 1) * 128,
                    (n % 4) * 2048:(n % 4 + 1) * 2048]

    P = {"ld": 0, "vp": 0, "dt": 0, "st": 0, "vps": 0}

    def prep_region():
        P["vps"] = P["vp"]

    def prep_gp(g, nloads, nstores, plan=0, tight=False):
        k = 0
        while P["st"] < min(P["dt"], NPREP) and k < nstores:
            j = P["st"]
            g.wait_ge(tprep.h, 16 * (j + 1))
            wsst.pdma(g, prep_dst(j), gbuf[j % 2][:])
            P["st"] += 1
            k += 1
        k = 0
        if tight:
            cap = min(2 * P["vps"] + 2, max(P["dt"] + 2, 2), NPREP)
        else:
            cap = min(2 * (P["vp"] + plan) + 2, max(P["st"] + 4, 4), NPREP)
        while P["ld"] < cap and k < nloads:
            j = P["ld"]
            if j >= 2:
                if tight:
                    g.wait_ge(tprep.h, 16 * (j - 1))
                g.wait_ge(gq.h, j - 1)
            wgld[j % 2].dma(g, wwb[j % 2][:], prep_src(j))
            P["ld"] += 1
            k += 1

    def prep_vec(v, npairs):
        k = 0
        while P["vp"] < min(P["ld"] // 2, NPREP // 2) and k < npairs:
            pk = P["vp"]
            a, b = 2 * pk, 2 * pk + 1
            v.wait_ge(wgld[0].h, 16 * (pk + 1))
            v.wait_ge(wgld[1].h, 16 * (pk + 1))
            if pk >= 1:
                v.wait_ge(tprep.h, 16 * (2 * pk))
            hb.tick(v.drain())
            sa = wwb[0][:].rearrange("p (g k) -> p g k", k=32)
            sbv = wwb[1][:].rearrange("p (g k) -> p g k", k=32)
            v.tensor_reduce(amq2[:, 0:64], sa, axis=AX.X, op=AluOpType.max,
                            apply_absolute_value=True)
            v.tensor_reduce(amq2[:, 64:128], sbv, axis=AX.X, op=AluOpType.max,
                            apply_absolute_value=True)
            hb.tick(v.drain())
            v.tensor_scalar(s2q2[:], amq2[:], AMAX_FLOOR, INV224,
                            AluOpType.max, AluOpType.mult)
            hb.tick(v.drain())
            v.reciprocal(ivq2[:], s2q2[:])
            hb.tick(v.drain())
            v.tensor_tensor(q8a[:].rearrange("p (g k) -> p g k", k=32), sa,
                            ivq2[:, 0:64].unsqueeze(2).broadcast_to([128, 64, 32]),
                            AluOpType.mult)
            v.tensor_tensor(q8b[:].rearrange("p (g k) -> p g k", k=32), sbv,
                            ivq2[:, 64:128].unsqueeze(2).broadcast_to([128, 64, 32]),
                            AluOpType.mult)
            hb.tick(v.drain())
            v.tensor_tensor(gdqb[:].rearrange("p (g k) -> p g k", k=32),
                            q8a[:].rearrange("p (g k) -> p g k", k=32),
                            s2q2[:, 0:64].unsqueeze(2).broadcast_to([128, 64, 32]),
                            AluOpType.mult)
            v.tensor_tensor(udqb[:].rearrange("p (g k) -> p g k", k=32),
                            q8b[:].rearrange("p (g k) -> p g k", k=32),
                            s2q2[:, 64:128].unsqueeze(2).broadcast_to([128, 64, 32]),
                            AluOpType.mult)
            hb.tick(v.drain())
            gq.tick(v.engine_nop())
            gq.tick(v.engine_nop())
            P["vp"] += 1
            k += 1

    def prep_sync(sy, nd, nloads=2):
        k = 0
        while P["dt"] < min(2 * P["vp"], P["st"] + 2, NPREP) and k < nd:
            j = P["dt"]
            sy.wait_ge(gq.h, j + 1)
            if j >= 2:
                sy.wait_ge(wsst.h, 16 * (j - 1))
            src = gdqb if j % 2 == 0 else udqb
            dmat(sy, tprep, gbuf[j % 2][:].rearrange("p (c d) -> p c d", d=128),
                 src[:])
            P["dt"] += 1
            k += 1


    HB1 = {}
    KA = [0]

    def ka_run(te, target):
        while KA[0] < target:
            te.wait_ge(hb.h, KA[0] + 1)
            te.matmul(ps[6][0:1, 0:1], ones_row[0:1, 0:1],
                      ones_row[0:1, 0:1], start=True, stop=True)
            KA[0] += 1

    p1s = ExitStack()
    xnTh = sb(p1s, "xnTh", [128, HC, 1024], BF16)
    wkb = [sb(p1s, f"wkb{p}", [128, HC, 256], BF16) for p in range(2)]
    wvb = [sb(p1s, f"wvb{p}", [128, HC, 256], BF16) for p in range(2)]
    kevb = [sb(p1s, f"kevb{p}", [128, 512], BF16) for p in range(2)]
    vevb = [sb(p1s, f"vevb{p}", [128, 256], BF16) for p in range(2)]

    kst_cnt = [0]
    vst_cnt = [0]
    NDG = 8
    XNT_AT = {}

    def kv_wload(g, qq, dg):
        bg = qq * NDG + dg
        par = bg % 2
        if bg >= 2:
            g.wait_ge(kmm.h, (bg - 1) * 4)
            g.wait_ge(vmm.h, (bg - 1) * 8)
        kvwld[par].dma(g, wkb[par][:],
                       wk[:, dg * 256:(dg + 1) * 256]
                       .rearrange("(c p) d -> p c d", p=128))
        kvwld[par].dma(g, wvb[par][:],
                       wv[:, dg * 256:(dg + 1) * 256]
                       .rearrange("(c p) d -> p c d", p=128))

    for qq in range(4):
        base = qq * 8
        prep_region()

        @blk.gpsimd
        def _(g, qq=qq):
            for dg in range(2):
                kv_wload(g, qq, dg)
            prep_gp(g, 4, 0)

        ln_block(8, base, xb, qq * 1024,
                 norm_out=lambda i: xnbf[i % 2],
                 norm_guards=lambda i: [(tdone, 16 * (i - 1))])

        @blk.vector
        def _(v, qq=qq):
            prep_vec(v, 8)

        @blk.sync
        def _(sy, qq=qq):
            if qq >= 1:
                sy.wait_ge(kmm.h, 32 * qq)
                sy.wait_ge(vmm.h, 64 * qq)
            for t in range(8):
                i = qq * 8 + t
                sy.wait_ge(snorm.h, 2 * i + 2)
                XNT_AT[i] = dmat(sy, tdone,
                                 xnTh[:, :, t * 128:(t + 1) * 128],
                                 xnbf[i % 2][:])

        for dg in range(NDG):
            prep_region()

            @blk.gpsimd
            def _(g, qq=qq, dg=dg):
                for dtl in range(2):
                    dt = dg * 2 + dtl
                    for w in range(2):
                        kt = kst_cnt[0]
                        kst_cnt[0] += 1
                        g.wait_ge(kev.h, kt + 1)
                        kst[kt % 2].pdma(
                            g, kT_s[dt * 128:(dt + 1) * 128,
                                    qq * 1024 + w * 512:qq * 1024 + (w + 1) * 512],
                            kevb[kt % 2][:])
                for tt in range(8):
                    vt = vst_cnt[0]
                    vst_cnt[0] += 1
                    g.wait_ge(vev.h, vt + 1)
                    vst[vt % 2].pdma(
                        g, v_s[(qq * 8 + tt) * 128:(qq * 8 + tt + 1) * 128,
                               dg * 256:(dg + 1) * 256],
                        vevb[vt % 2][:])
                if dg + 2 < NDG:
                    kv_wload(g, qq, dg + 2)
                prep_gp(g, 4, 4, plan=2)

            @blk.vector
            def _(v, qq=qq, dg=dg):
                prep_vec(v, 2)
                HB1[(qq, dg)] = hb.n

            @blk.scalar
            def _(sc, qq=qq, dg=dg):
                for j in range(4):
                    kt = kev.n
                    sc.wait_ge(kmm.h, kt + 1)
                    if kt >= 2:
                        sc.wait_ge(kst[kt % 2].h, 16 * (kt // 2))
                    kev.tick(sc.activation(kevb[kt % 2][:], ps[kt % 2][:],
                                           AF.Copy))
                for j in range(8):
                    vt = vev.n
                    sc.wait_ge(vmm.h, vt + 1)
                    if vt >= 2:
                        sc.wait_ge(vst[vt % 2].h, 16 * (vt // 2))
                    vev.tick(sc.activation(vevb[vt % 2][:],
                                           ps[2 + vt % 2][:, 0:256], AF.Copy))

            @blk.sync
            def _(sy):
                prep_sync(sy, 4)

            @blk.gpsimd
            def _(g, qq=qq, dg=dg):
                prep_gp(g, 2, 2)

            @blk.sync
            def _(sy):
                prep_sync(sy, 2)

        @blk.tensor
        def _(te, qq=qq):
            if qq == 0:
                te.wait_ge(cst.h, 64)
                te.wait_ge(vstat.h, ONES_AT)
            for dg in range(NDG):
                bg = qq * NDG + dg
                par = bg % 2
                if dg >= 1:
                    ka_run(te, HB1[(qq, dg - 1)])
                te.wait_ge(kvwld[par].h, 32 * (bg // 2 + 1))
                for dtl in range(2):
                    dt = dg * 2 + dtl
                    for w in range(2):
                        kt = kmm.n
                        bank = ps[kt % 2]
                        if kt >= 2:
                            te.wait_ge(kev.h, kt - 1)
                        te.wait_ge(tdone.h, XNT_AT[qq * 8 + 4 * (w + 1) - 1])
                        for hc in range(HC):
                            te.matmul(bank[:],
                                      wkb[par][:, hc, dtl * 128:(dtl + 1) * 128],
                                      xnTh[:, hc, w * 512:(w + 1) * 512],
                                      start=(hc == 0), stop=False)
                        kmm.tick(te.matmul(bank[:],
                                           bias_sb[1][0:1, dt * 128:(dt + 1) * 128],
                                           ones_bf[0:1, :],
                                           start=False, stop=True))
                for tt in range(8):
                    vt = vmm.n
                    bank = ps[2 + vt % 2]
                    if vt >= 2:
                        te.wait_ge(vev.h, vt - 1)
                    te.wait_ge(tdone.h, XNT_AT[qq * 8 + tt])
                    for hc in range(HC):
                        te.matmul(bank[:, 0:256],
                                  xnTh[:, hc, tt * 128:(tt + 1) * 128],
                                  wvb[par][:, hc, :],
                                  start=(hc == 0), stop=False)
                    vmm.tick(te.matmul(bank[:, 0:256],
                                       ones_bf[0:1, 0:128],
                                       bias_sb[2][0:1, dg * 256:(dg + 1) * 256],
                                       start=False, stop=True))


    prep_region()

    @blk.gpsimd
    def _(g):
        g.wait_ge(kmm.h, 128)
        g.wait_ge(vmm.h, 256)
        for dg in range(2):
            qwld[dg % 2].dma(g, wkb[dg % 2][:],
                             wq[:, dg * 256:(dg + 1) * 256]
                             .rearrange("(c p) d -> p c d", p=128))

    ln_block(8, 32, xo, 0,
             norm_out=lambda i: xnbf[i % 2],
             norm_guards=lambda i: [(tdone, 16 * (i - 1))])

    @blk.gpsimd
    def _(g):
        for dg in range(2, NDG):
            par = dg % 2
            g.wait_ge(qmm.h, (dg - 1) * 4)
            qwld[par].dma(g, wkb[par][:],
                          wq[:, dg * 256:(dg + 1) * 256]
                          .rearrange("(c p) d -> p c d", p=128))
        prep_gp(g, 8, 8, plan=2)

    @blk.vector
    def _(v):
        prep_vec(v, 6)

    @blk.sync
    def _(sy):
        sy.wait_ge(kmm.h, 128)
        sy.wait_ge(vmm.h, 256)
        for t in range(8):
            i = 32 + t
            sy.wait_ge(snorm.h, 2 * i + 2)
            XNT_AT[i] = dmat(sy, tdone, xnTh[:, :, t * 128:(t + 1) * 128],
                             xnbf[i % 2][:])
        prep_sync(sy, 6)

    @blk.tensor
    def _(te):
        for dg in range(NDG):
            par = dg % 2
            te.wait_ge(qwld[par].h, 16 * (dg // 2 + 1))
            for dtl in range(2):
                dt = dg * 2 + dtl
                for w in range(2):
                    qt = qmm.n
                    bank = ps[4 + qt % 2]
                    if qt >= 2:
                        te.wait_ge(qev.h, qt - 1)
                    te.wait_ge(tdone.h, XNT_AT[32 + 4 * (w + 1) - 1])
                    for hc in range(HC):
                        te.matmul(bank[:],
                                  wkb[par][:, hc, dtl * 128:(dtl + 1) * 128],
                                  xnTh[:, hc, w * 512:(w + 1) * 512],
                                  start=(hc == 0), stop=False)
                    qmm.tick(te.matmul(bank[:],
                                       bias_sb[0][0:1, dt * 128:(dt + 1) * 128],
                                       ones_bf[0:1, :],
                                       start=False, stop=True))

    @blk.scalar
    def _(sc):
        qcnt = [0]
        for dg in range(NDG):
            for dtl in range(2):
                dt = dg * 2 + dtl
                for w in range(2):
                    qt = qcnt[0]
                    qcnt[0] += 1
                    sc.wait_ge(qmm.h, qt + 1)
                    qev.tick(sc.activation(QT[:, dt, w * 512:(w + 1) * 512],
                                           ps[4 + qt % 2][:], AF.Copy))

    p1s.close()
    xnbfS.close()

    p2s = ExitStack()
    krow = [[sb(p2s, f"krow{s_}_{i}", [128, S], BF16) for i in range(2)]
            for s_ in range(2)]
    vchunk = [sb(p2s, f"vchunk{p}", [128, 2 * 128], BF16) for p in range(2)]
    PT = [sb(p2s, f"PT{p}", [128, 1024], BF16) for p in range(2)]
    izv = sb(p2s, "izv", [1, 1024], F32)
    izb = sb(p2s, "izb", [128, 1024], F32)

    def krld_target(p):
        st = p % 2
        return 16 * sum(GLEN[r] for r in range(p + 1) if r % 2 == st)

    AVCUM = []
    for p, hs in enumerate(PASSES):
        for kc in range(32):
            AVCUM.append(2 * (PT0[p] + (kc + 1) * len(hs)))

    for p, hs in enumerate(PASSES):
        prep_region()

        for sub in range(8):
            prep_region()

            @blk.gpsimd
            def _(g, p=p, hs=hs, sub=sub):
                st = p % 2
                if p == 0 and sub == 0:
                    g.wait_ge(qev.h, 32)
                    for pp in range(2):
                        g.wait_ge(kst[pp].h, kst[pp].n)
                        g.wait_ge(vst[pp].h, vst[pp].n)
                if sub == 0:
                    if p >= 2:
                        g.wait_ge(smm.h, 2 * PT0[p - 1])
                    for i, h in enumerate(hs):
                        krld[st].dma(g, krow[st][i][:],
                                     kT_s[h * 128:(h + 1) * 128, :])
                for kc in range(sub * 4, sub * 4 + 4):
                    n = p * 32 + kc
                    if n >= 2:
                        g.wait_ge(ptav.h, AVCUM[n - 2])
                    d0 = hs[0] * 128
                    vcld[n % 2].dma(g, vchunk[n % 2][:, 0:len(hs) * 128],
                                    v_s[kc * 128:(kc + 1) * 128,
                                        d0:d0 + len(hs) * 128])
                prep_gp(g, 4, 4, plan=2)

            if sub % 2 == 1:
                @blk.vector
                def _(v, p=p, sub=sub):
                    prep_vec(v, 1)

            @blk.sync
            def _(sy, p=p, sub=sub):
                prep_sync(sy, 4)

        @blk.tensor
        def _(te, p=p, hs=hs):
            st = p % 2
            gl = len(hs)
            if p == 0:
                te.wait_ge(qev.h, 32)
            te.wait_ge(krld[st].h, krld_target(p))
            te.wait_ge(oev.h, 2 * GH0[p])
            for kc in range(32):
                n = p * 32 + kc
                te.wait_ge(vcld[n % 2].h, 16 * (n // 2 + 1))
                for i in range(gl):
                    ptn = PT0[p] + kc * gl + i
                    for qh in range(2):
                        scn = 2 * ptn + qh
                        if scn >= 2:
                            te.wait_ge(sexp.h, scn - 1)
                        smm.tick(te.matmul(ps[qh][:],
                                           krow[st][i][:, kc * 128:(kc + 1) * 128],
                                           QT[:, hs[i], qh * 512:(qh + 1) * 512],
                                           start=True, stop=True))
                for i in range(gl):
                    ptn = PT0[p] + kc * gl + i
                    for qh in range(2):
                        te.wait_ge(sexp.h, 2 * ptn + qh + 1)
                        te.matmul(ps[2 + 2 * i + qh][:],
                                  vchunk[n % 2][:, i * 128:(i + 1) * 128],
                                  PT[ptn % 2][:, qh * 512:(qh + 1) * 512],
                                  start=(kc == 0), stop=(kc == 31))
                        ptav.tick(te.matmul(
                            ps[6 + qh][32 * i:32 * i + 1, :], ones_cbf[:],
                            PT[ptn % 2][:, qh * 512:(qh + 1) * 512],
                            start=(kc == 0), stop=(kc == 31)))
            te.wait_ge(sexp.h, 2 * PT0[p + 1])
            for i in range(gl):
                gh = GH0[p] + i
                if i > 0:
                    te.wait_ge(oev.h, 2 * gh)
                te.wait_ge(vzf.h, gh + 1)
                for qh in range(2):
                    bcmm.tick(te.matmul(ps[qh][:], ones_row[0:1, 0:128],
                                        izv[0:1, qh * 512:(qh + 1) * 512],
                                        start=True, stop=True))

        @blk.scalar
        def _(sc, p=p, hs=hs):
            gl = len(hs)
            for kc in range(32):
                for i in range(gl):
                    ptn = PT0[p] + kc * gl + i
                    for qh in range(2):
                        scn = 2 * ptn + qh
                        sc.wait_ge(smm.h, scn + 1)
                        if ptn >= 2:
                            sc.wait_ge(ptav.h, 2 * (ptn - 1))
                        sexp.tick(sc.activation(
                            PT[ptn % 2][:, qh * 512:(qh + 1) * 512],
                            ps[qh][:], AF.Exp, scale=SCL))

        @blk.vector
        def _(v, p=p, hs=hs):
            gl = len(hs)
            for i in range(gl):
                gh = GH0[p] + i
                v.wait_ge(bcmm.h, 2 * gh)
                v.wait_ge(ptav.h, 2 * (PT0[p] + 31 * gl + i + 1))
                v.reciprocal(izv[0:1, 0:512], ps[6][32 * i:32 * i + 1, :])
                v.reciprocal(izv[0:1, 512:1024], ps[7][32 * i:32 * i + 1, :])
                vzf.tick(v.drain())
                v.wait_ge(bcmm.h, 2 * gh + 2)
                v.tensor_copy(izb[:, 0:512], ps[0][:])
                v.tensor_copy(izb[:, 512:1024], ps[1][:])
                v.drain()
                for qh in range(2):
                    v.tensor_tensor(AVT[:, hs[i], qh * 512:(qh + 1) * 512],
                                    ps[2 + 2 * i + qh][:],
                                    izb[:, qh * 512:(qh + 1) * 512],
                                    AluOpType.mult)
                    oev.tick(v.drain())

        @blk.vector
        def _(v, p=p):
            prep_vec(v, 2)

    p2s.close()
    QTs.close()

    p3s = ExitStack()
    woblk = [sb(p3s, f"woblk{og}", [128, HC, 512], BF16) for og in range(4)]
    outb = [sb(p3s, f"outb{p}", [128, H], F32) for p in range(2)]
    bo_sb = sb(p3s, "bo_sb", [1, H], BF16)
    bost = sem("bost")

    XLD3 = []
    _x0, _x1 = xld[0].n, xld[1].n
    for tt in range(8):
        if tt % 2 == 0:
            _x0 += 16
            XLD3.append(_x0)
        else:
            _x1 += 16
            XLD3.append(_x1)

    for tt in range(8):
        prep_region()

        @blk.gpsimd
        def _(g, tt=tt):
            par = tt % 2
            if tt == 0:
                g.wait_ge(oev.h, 32)
                bost.dma(g, bo_sb[:], bias4[3:4, :])
                for og in range(4):
                    wowld.dma(g, woblk[og][:],
                              wo[:, og * 512:(og + 1) * 512]
                              .rearrange("(c p) d -> p c d", p=128))
                g.wait_ge(snorm.h, 80)
            if tt >= 2:
                g.wait_ge(p3ev.h, 4 * (tt - 1))
            xld[par].dma(g, xbuf[par][:], xo[tt * 128:(tt + 1) * 128, :])
            if tt >= 2:
                g.wait_ge(p3ev.h, 4 * (tt - 1))
                ost[tt % 2].pdma(g, out[(tt - 2) * 128:(tt - 1) * 128, :],
                                 outb[tt % 2][:])
            prep_gp(g, 4, 4, plan=2)

        @blk.vector
        def _(v, tt=tt):
            par = tt % 2
            v.wait_ge(xld[par].h, XLD3[tt])
            if tt >= 2:
                v.wait_ge(ost[par].h, 16 * (tt // 2))
            for og in range(4):
                nn = tt * 4 + og
                v.wait_ge(p3mm.h, nn + 1)
                v.drain()
                v.tensor_tensor(outb[par][:, og * 512:(og + 1) * 512],
                                ps[nn % 2][:],
                                xbuf[par][:, og * 512:(og + 1) * 512],
                                AluOpType.add)
                p3ev.tick(v.drain())
            prep_vec(v, 1)

        @blk.gpsimd
        def _(g, tt=tt):
            prep_gp(g, 2, 2, plan=1)

        @blk.vector
        def _(v, tt=tt):
            prep_vec(v, 1)

        @blk.sync
        def _(sy, tt=tt):
            prep_sync(sy, 6, 4)

    @blk.gpsimd
    def _(g):
        for tt in range(6, 8):
            g.wait_ge(p3ev.h, 4 * (tt + 1))
            ost[tt % 2].pdma(g, out[tt * 128:(tt + 1) * 128, :], outb[tt % 2][:])

    @blk.tensor
    def _(te):
        te.wait_ge(oev.h, 32)
        te.wait_ge(wowld.h, 64)
        te.wait_ge(bost.h, 16)
        for tt in range(8):
            for og in range(4):
                nn = tt * 4 + og
                bank = ps[nn % 2]
                if nn >= 2:
                    te.wait_ge(p3ev.h, nn - 1)
                for hc in range(HC):
                    te.matmul(bank[:],
                              AVT[:, hc, tt * 128:(tt + 1) * 128],
                              woblk[og][:, hc, :],
                              start=(hc == 0), stop=False)
                p3mm.tick(te.matmul(bank[:],
                                    ones_bf[0:1, 0:128],
                                    bo_sb[0:1, og * 512:(og + 1) * 512],
                                    start=False, stop=True))

    @blk.sync
    def _(sy):
        prep_sync(sy, 999)


    p3s.close()
    AVTs.close()
    xbufS.close()
    biasS.close()

    p4s = ExitStack()
    xdqT = sb(p4s, "xdqT", [128, HC, TO], BF16)
    xwb = [sb(p4s, f"xwb{p}", [128, H], F32) for p in range(2)]
    xdq = [sb(p4s, f"xdq{p}", [128, H], BF16) for p in range(2)]
    gbufL = [sb(p4s, f"gbufL{p}", [128, H], BF16) for p in range(2)]
    ubufL = [sb(p4s, f"ubufL{p}", [128, H], BF16) for p in range(2)]
    sgb = [sb(p4s, f"sgb{p}", [128, 512], F32) for p in range(2)]
    hfb = sb(p4s, "hfb", [128, 512], F32)
    htt = sb(p4s, "htt", [128, 512], F32)
    amh = sb(p4s, "amh", [128, 16], F32)
    s2h = sb(p4s, "s2h", [128, 16], F32)
    ivh = sb(p4s, "ivh", [128, 16], F32)
    h8b = sb(p4s, "h8b", [128, 512], FP8)
    hdt = sb(p4s, "hdt", [128, 512], BF16)
    hdq = sb(p4s, "hdq", [128, 64 * 512], BF16)
    dbufp = [sb(p4s, f"dbufp{p}", [128, H], BF16) for p in range(2)]
    ftb = [sb(p4s, f"ftb{p}", [128, 512], F32) for p in range(2)]
    tbuf = [sb(p4s, f"tbuf{p}", [128, 128], F32) for p in range(2)]

    TD_P4A = tdone.n

    @blk.gpsimd
    def _(g):
        for t in range(8):
            i = 40 + t
            par = i % 2
            if t == 0:
                g.wait_ge(p3ev.h, 32)
                g.wait_ge(p3mm.h, 32)
                for pp in range(2):
                    g.wait_ge(ost[pp].h, ost[pp].n)
            if t >= 2:
                g.wait_ge(xq.h, t - 1)
            XLD_AT[i] = xld[par].dma(g, xwb[par][:], xo[t * 128:(t + 1) * 128, :])
            prep_gp(g, 2, 2)

    def p4a_quant(v, t):
        par = (40 + t) % 2
        v.wait_ge(snorm.h, 2 * (40 + t) + 2)
        if t >= 2:
            v.wait_ge(tdone.h, TD_P4A + 16 * (t - 1))
        emit_quant(v, xwb[par], H, amq2, s2q2, ivq2, q8a, xdq[par])
        xq.tick(v.engine_nop())

    @blk.vector
    def _(v):
        for t in range(8):
            prep_region()
            ln_vec_ab(v, 40 + t, bufs=xwb)
            if t >= 1:
                p4a_quant(v, t - 1)
        p4a_quant(v, 7)

    @blk.scalar
    def _(sc):
        for t in range(8):
            i = 40 + t
            ln_scalar(sc, i, xwb[i % 2], bufs=xwb)

    @blk.sync
    def _(sy):
        for t in range(8):
            sy.wait_ge(xq.h, t + 1)
            dmat(sy, tdone, xdqT[:, :, t * 128:(t + 1) * 128], xdq[t % 2][:])
            prep_sync(sy, 3)

    NCH = 4

    def emit_hpipe(v, th, It):
        gg = th * 64 + It
        v.wait_ge(hsil.h, gg + 1)
        v.drain()
        v.tensor_tensor(hfb[:], sgb[gg % 2][:], ps[2 + gg % 2][:],
                        AluOpType.mult)
        v.drain()
        v.transpose(htt[:], hfb[:])
        v.drain()
        ht3 = htt[:].rearrange("p (g k) -> p g k", k=32)
        v.tensor_reduce(amh[:], ht3, axis=AX.X, op=AluOpType.max,
                        apply_absolute_value=True)
        v.drain()
        v.tensor_scalar(s2h[:], amh[:], AMAX_FLOOR, INV224,
                        AluOpType.max, AluOpType.mult)
        v.drain()
        v.reciprocal(ivh[:], s2h[:])
        v.drain()
        v.tensor_tensor(h8b[:].rearrange("p (g k) -> p g k", k=32), ht3,
                        ivh[:].unsqueeze(2).broadcast_to([128, 16, 32]),
                        AluOpType.mult)
        v.drain()
        v.tensor_tensor(hdt[:].rearrange("p (g k) -> p g k", k=32),
                        h8b[:, 0:512].rearrange("p (g k) -> p g k", k=32),
                        s2h[:].unsqueeze(2).broadcast_to([128, 16, 32]),
                        AluOpType.mult)
        v.drain()
        v.transpose(hdq[:, It * 512:(It + 1) * 512], hdt[:])
        hq.tick(v.drain())

    def emit_tcopy(v, th, gg):
        for tq in range(4):
            m = 4 * gg + tq
            v.wait_ge(ttr.h, 4 * gg + 4)
            if m >= 2:
                v.wait_ge(accst[m % 2].h, 16 * (m // 2))
            v.tensor_copy(tbuf[m % 2][:],
                          ps[6 + gg % 2][:, tq * 128:(tq + 1) * 128])
            tbe.tick(v.drain())

    def emit_accums(g, gg):
        for tq in range(4):
            m = 4 * gg + tq
            g.wait_ge(tbe.h, m + 1)
            th_, ho_ = gg // 16, gg % 16
            accst[m % 2].pdma(
                g, out[(th_ * 4 + tq) * 128:(th_ * 4 + tq + 1) * 128,
                       ho_ * 128:(ho_ + 1) * 128],
                tbuf[m % 2][:], accum_op=AluOpType.add)

    for th in range(2):
        for It in range(64):
            prep_region()

            @blk.gpsimd
            def _(g, th=th, It=It):
                par = It % 2
                gg = th * 64 + It
                prep_gp(g, 2, 2, tight=True)
                assert P["st"] >= 2 * It + 2, (P, th, It)
                if gg >= 2:
                    g.wait_ge(p4mm.h, gg - 1)
                g.wait_ge(wsst.h, 16 * (2 * It + 2))
                gsld[par].dma(g, gbufL[par][:], gT_s[It * 128:(It + 1) * 128, :])
                gsld[par].dma(g, ubufL[par][:], uT_s[It * 128:(It + 1) * 128, :])

            @blk.vector
            def _(v, th=th, It=It):
                if th == 1 and It == 0:
                    v.wait_ge(p4dpc.h, 64)
                emit_hpipe(v, th, It)
                prep_vec(v, 1)

            @blk.sync
            def _(sy, th=th, It=It):
                prep_sync(sy, 2)

        @blk.tensor
        def _(te, th=th):
            if th == 0:
                te.wait_ge(p3ev.h, 32)
                te.wait_ge(tdone.h, TD_P4A + 16 * 8)
            for It in range(64):
                par = It % 2
                gg = th * 64 + It
                gb = gbufL[par][:].rearrange("p (c d) -> p c d", d=128)
                ub = ubufL[par][:].rearrange("p (c d) -> p c d", d=128)
                te.wait_ge(gsld[par].h, 32 * (th * 32 + It // 2 + 1))
                if gg >= 2:
                    te.wait_ge(hsil.h, gg - 1)
                    te.wait_ge(hq.h, gg - 1)
                for hc in range(HC):
                    te.matmul(ps[gg % 2][:], gb[:, hc, :],
                              xdqT[:, hc, th * 512:(th + 1) * 512],
                              start=(hc == 0), stop=(hc == HC - 1))
                for hc in range(HC):
                    ii = te.matmul(ps[2 + gg % 2][:], ub[:, hc, :],
                                   xdqT[:, hc, th * 512:(th + 1) * 512],
                                   start=(hc == 0), stop=(hc == HC - 1))
                p4mm.tick(ii)

        @blk.scalar
        def _(sc, th=th):
            for It in range(64):
                gg = th * 64 + It
                sc.wait_ge(p4mm.h, gg + 1)
                if gg >= 2:
                    sc.wait_ge(hq.h, gg - 1)
                hsil.tick(sc.activation(sgb[gg % 2][:], ps[gg % 2][:], AF.Silu))

        for ho in range(16):
            prep_region()

            @blk.gpsimd
            def _(g, th=th, ho=ho):
                prep_gp(g, 4, 4, tight=True)
                for ch in range(NCH):
                    n = ho * NCH + ch
                    gp = th * 64 + n
                    if gp >= 2:
                        g.wait_ge(p4dpc.h, gp - 1)
                    if th == 0:
                        assert P["st"] >= 129 + n, (P, th, ho, ch, n)
                        g.wait_ge(wsst.h, 16 * (129 + n))
                    dsld[gp % 2].dma(g, dbufp[gp % 2][:],
                                     dT_s[ho * 128:(ho + 1) * 128,
                                          ch * 2048:(ch + 1) * 2048])
                if ho >= 2:
                    emit_accums(g, th * 16 + ho - 2)
                if ho == 15:
                    for gg in range(th * 16 + 14, th * 16 + 16):
                        emit_accums(g, gg)

            @blk.vector
            def _(v, th=th, ho=ho):
                emit_tcopy(v, th, th * 16 + ho)
                prep_vec(v, 1)

            @blk.sync
            def _(sy, th=th, ho=ho):
                prep_sync(sy, 4, 4)

            @blk.gpsimd
            def _(g, th=th, ho=ho):
                prep_gp(g, 2, 2, tight=True)

            @blk.vector
            def _(v, th=th, ho=ho):
                prep_vec(v, 1)

            @blk.sync
            def _(sy, th=th, ho=ho):
                prep_sync(sy, 4, 4)

        @blk.tensor
        def _(te, th=th):
            te.wait_ge(hq.h, 64 * (th + 1))

            def down_mm(ho):
                gg = th * 16 + ho
                if gg >= 2:
                    te.wait_ge(fts.h, gg - 1)
                for ch in range(NCH):
                    n = ho * NCH + ch
                    gp = th * 64 + n
                    te.wait_ge(dsld[gp % 2].h, 16 * (gp // 2 + 1))
                    for icl in range(16):
                        ic = ch * 16 + icl
                        ii = te.matmul(ps[4 + gg % 2][:],
                                       dbufp[gp % 2][:, icl * 128:(icl + 1) * 128],
                                       hdq[:, ic * 512:(ic + 1) * 512],
                                       start=(ic == 0), stop=(ic == 63))
                    p4dpc.tick(ii)

            def transp(ho):
                gg = th * 16 + ho
                te.wait_ge(fts.h, gg + 1)
                if gg >= 2:
                    te.wait_ge(tbe.h, 4 * (gg - 1))
                for tq in range(4):
                    ttr.tick(te.transpose(
                        ps[6 + gg % 2][:, tq * 128:(tq + 1) * 128],
                        ftb[gg % 2][:, tq * 128:(tq + 1) * 128],
                        ident_sb[:]))

            for ho in range(16):
                down_mm(ho)
                if ho >= 2:
                    transp(ho - 2)
            transp(14)
            transp(15)

        @blk.scalar
        def _(sc, th=th):
            for ho in range(16):
                gg = th * 16 + ho
                sc.wait_ge(p4dpc.h, 4 * gg + 4)
                if gg >= 2:
                    sc.wait_ge(ttr.h, 4 * (gg - 1))
                fts.tick(sc.activation(ftb[gg % 2][:], ps[4 + gg % 2][:], AF.Copy))

    assert P["ld"] == NPREP and P["vp"] * 2 == NPREP
    assert P["dt"] == NPREP and P["st"] == NPREP

    @blk.gpsimd
    def _(g):
        for pp in range(2):
            g.wait_ge(accst[pp].h, accst[pp].n)
            g.wait_ge(ost[pp].h, ost[pp].n)

    p4s.close()
    prepS.close()
    es.close()
    return nc


def kernel(L0_attn_hidden_state, L0_attn_Wq, L0_attn_bq, L0_attn_Wk, L0_attn_bk,
           L0_attn_Wv, L0_attn_bv, L0_attn_Wo, L0_attn_bo,
           L0_ffn_gate_w, L0_ffn_up_w, L0_ffn_down_w):
    x = np.ascontiguousarray(np.asarray(L0_attn_hidden_state, dtype=np.float32))
    Wq = np.ascontiguousarray(np.asarray(L0_attn_Wq, dtype=np.float32))
    Wk = np.ascontiguousarray(np.asarray(L0_attn_Wk, dtype=np.float32))
    Wv = np.ascontiguousarray(np.asarray(L0_attn_Wv, dtype=np.float32))
    Wo = np.ascontiguousarray(np.asarray(L0_attn_Wo, dtype=np.float32))
    bias = np.ascontiguousarray(np.stack([
        np.asarray(L0_attn_bq, dtype=np.float32),
        np.asarray(L0_attn_bk, dtype=np.float32),
        np.asarray(L0_attn_bv, dtype=np.float32),
        np.asarray(L0_attn_bo, dtype=np.float32)]))
    Gw = np.ascontiguousarray(np.asarray(L0_ffn_gate_w, dtype=np.float32))
    Uw = np.ascontiguousarray(np.asarray(L0_ffn_up_w, dtype=np.float32))
    Dw = np.ascontiguousarray(np.asarray(L0_ffn_down_w, dtype=np.float32))
    ident = np.eye(128, dtype=np.float32)

    if "nc" not in _CACHE:
        _CACHE["nc"] = build_program()
    nc = _CACHE["nc"]

    in_maps = []
    for c in range(8):
        b, sl = c // 4, c % 4
        in_maps.append({
            "xb": x[b],
            "xo": np.ascontiguousarray(x[b, sl * TO:(sl + 1) * TO]),
            "wq": Wq, "wk": Wk, "wv": Wv, "wo": Wo,
            "bias4": bias, "gw": Gw, "uw": Uw, "dw": Dw,
            "ident": ident,
        })
    res = run_bass_kernel_spmd(nc, in_maps, core_ids=list(range(8)))
    outp = np.empty((2, S, H), dtype=np.float32)
    for c in range(8):
        b, sl = c // 4, c % 4
        outp[b, sl * TO:(sl + 1) * TO] = np.asarray(res.results[c]["out"])
    return outp


# version 13
# speedup vs baseline: 1.0592x; 1.0498x over previous
import numpy as np
from contextlib import ExitStack

import concourse.bass as bass
import concourse.mybir as mybir
from concourse.alu_op_type import AluOpType
from concourse.bass_utils import run_bass_kernel_spmd

F32 = mybir.dt.float32
BF16 = mybir.dt.bfloat16
FP8 = mybir.dt.float8e4
AF = mybir.ActivationFunctionType
AX = mybir.AxisListType

S = 4096
TO = 1024
H = 2048
II = 8192
HC = 16
EPS = float(np.float32(1e-5))
SCL = float(np.float32(1.0) / np.float32(np.sqrt(np.float32(128.0))))
AMAX_FLOOR = float(np.float32(448e-30))
INV224 = float(np.float32(1.0) / np.float32(224.0))

PASSES = [[2 * i, 2 * i + 1] for i in range(8)]
GLEN = [len(p) for p in PASSES]
PT0 = [sum(g * 32 for g in GLEN[:p]) for p in range(9)]
GH0 = [sum(GLEN[:p]) for p in range(9)]

_CACHE = {}


class Sem:
    __slots__ = ("h", "n")

    def __init__(self, h):
        self.h = h
        self.n = 0

    def dma(self, eng, out, in_, **kw):
        eng.dma_start(out=out, in_=in_, **kw).then_inc(self.h, 16)
        self.n += 16
        return self.n

    def pdma(self, eng, out, in_, **kw):
        if self.n:
            eng.wait_ge(self.h, self.n)
        return self.dma(eng, out, in_, **kw)

    def tick(self, instr):
        instr.then_inc(self.h, 1)
        self.n += 1
        return self.n


def build_program():
    nc = bass.Bass("TRN2", target_bir_lowering=False, debug=False)
    es = ExitStack()

    def sem(name):
        return Sem(es.enter_context(nc.semaphore(name)))

    def sb(stack, name, shape, dt):
        return stack.enter_context(nc.sbuf_tensor(name, shape, dt))

    xb = nc.dram_tensor("xb", [S, H], F32, kind="ExternalInput").ap()
    xo = nc.dram_tensor("xo", [TO, H], F32, kind="ExternalInput").ap()
    wq = nc.dram_tensor("wq", [H, H], F32, kind="ExternalInput").ap()
    wk = nc.dram_tensor("wk", [H, H], F32, kind="ExternalInput").ap()
    wv = nc.dram_tensor("wv", [H, H], F32, kind="ExternalInput").ap()
    wo = nc.dram_tensor("wo", [H, H], F32, kind="ExternalInput").ap()
    bias4 = nc.dram_tensor("bias4", [4, H], F32, kind="ExternalInput").ap()
    gw = nc.dram_tensor("gw", [II, H], F32, kind="ExternalInput").ap()
    uw = nc.dram_tensor("uw", [II, H], F32, kind="ExternalInput").ap()
    dw = nc.dram_tensor("dw", [H, II], F32, kind="ExternalInput").ap()
    ident = nc.dram_tensor("ident", [128, 128], F32, kind="ExternalInput").ap()
    out = nc.dram_tensor("out", [TO, H], F32, kind="ExternalOutput").ap()

    kT_s = nc.dram_tensor("kT_s", [H, S], BF16, kind="Internal").ap()
    v_s = nc.dram_tensor("v_s", [S, H], BF16, kind="Internal").ap()
    gT_s = nc.dram_tensor("gT_s", [II, H], BF16, kind="Internal").ap()
    uT_s = nc.dram_tensor("uT_s", [II, H], BF16, kind="Internal").ap()
    dT_s = nc.dram_tensor("dT_s", [H, II], BF16, kind="Internal").ap()

    ps = [es.enter_context(nc.psum_tensor(f"ps{i}", [128, 512], F32))
          for i in range(8)]

    ident_sb = sb(es, "ident_sb", [128, 128], F32)
    ones_row = sb(es, "ones_row", [1, 128], F32)
    ones_bf = sb(es, "ones_bf", [1, 512], BF16)
    ones_col = sb(es, "ones_col", [128, 1], F32)
    ones_cbf = sb(es, "ones_cbf", [128, 1], BF16)
    st6 = [sb(es, f"st6_{p}", [128, 24], F32) for p in range(2)]
    ag = [sb(es, f"ag{p}", [128, 2], F32) for p in range(2)]
    sml = [sb(es, f"sml{p}", [128, 8], F32) for p in range(2)]
    q8a = sb(es, "q8a", [128, H], FP8)
    q8b = sb(es, "q8b", [128, H], FP8)
    amq2 = sb(es, "amq2", [128, 128], F32)
    s2q2 = sb(es, "s2q2", [128, 128], F32)
    ivq2 = sb(es, "ivq2", [128, 128], F32)

    prepS = ExitStack()
    wwb = [sb(prepS, f"wwb{p}", [128, H], F32) for p in range(2)]
    gdqb = sb(prepS, "gdqb", [128, H], BF16)
    udqb = sb(prepS, "udqb", [128, H], BF16)
    gbuf = [sb(prepS, f"gbuf{p}", [128, H], BF16) for p in range(2)]

    biasS = ExitStack()
    bias_sb = [sb(biasS, f"bias_sb{i}", [1, H], BF16) for i in range(3)]
    xbufS = ExitStack()
    xbuf = [sb(xbufS, f"xbuf{p}", [128, H], F32) for p in range(2)]
    AVTs = ExitStack()
    AVT = sb(AVTs, "AVT", [128, HC, TO], BF16)
    QTs = ExitStack()
    QT = sb(QTs, "QT", [128, HC, TO], BF16)
    xnbfS = ExitStack()
    xnbf = [sb(xnbfS, f"xnbf{p}", [128, H], BF16) for p in range(2)]

    cst = sem("cst")
    xld = [sem(f"xld{p}") for p in range(2)]
    vstat = sem("vstat")
    snorm = sem("snorm")
    tdone = sem("tdone")
    tprep = sem("tprep")
    kvwld = [sem(f"kvwld{p}") for p in range(2)]
    qwld = [sem(f"qwld{p}") for p in range(2)]
    kmm = sem("kmm")
    vmm = sem("vmm")
    qmm = sem("qmm")
    kev = sem("kev")
    vev = sem("vev")
    qev = sem("qev")
    kst = [sem(f"kst{p}") for p in range(2)]
    vst = [sem(f"vst{p}") for p in range(2)]
    krld = [sem(f"krld{p}") for p in range(2)]
    vcld = [sem(f"vcld{p}") for p in range(4)]
    smm = sem("smm")
    sexp = sem("sexp")
    ptav = sem("ptav")
    vzf = sem("vzf")
    bcmm = sem("bcmm")
    oev = sem("oev")
    wowld = sem("wowld")
    p3mm = sem("p3mm")
    p3ev = sem("p3ev")
    ost = [sem(f"ost{p}") for p in range(2)]
    xq = sem("xq")
    wgld = [sem(f"wgld{p}") for p in range(2)]
    gq = sem("gq")
    wsst = sem("wsst")
    gsld = [sem(f"gsld{p}") for p in range(2)]
    p4mm = sem("p4mm")
    hsil = sem("hsil")
    hq = sem("hq")
    dsld = [sem(f"dsld{p}") for p in range(2)]
    p4dpc = sem("p4dpc")
    fts = sem("fts")
    ttr = sem("ttr")
    tbe = sem("tbe")
    accst = [sem(f"accst{p}") for p in range(2)]
    hb = sem("hb")

    blk = es.enter_context(nc.Block())

    def dmat(sy, dsem, out_ap, in_ap):
        if tdone.n:
            sy.wait_ge(tdone.h, tdone.n)
        if tprep.n:
            sy.wait_ge(tprep.h, tprep.n)
        sy.dma_start_transpose(out_ap, in_ap).then_inc(dsem.h, 16)
        dsem.n += 16
        return dsem.n

    @blk.gpsimd
    def _(g):
        for i in range(3):
            cst.dma(g, bias_sb[i][:], bias4[i:i + 1, :])
        cst.dma(g, ident_sb[:], ident)

    @blk.vector
    def _(v):
        v.memset(ones_row[:], 1.0)
        v.memset(ones_bf[:], 1.0)
        v.memset(ones_col[:], 1.0)
        v.memset(ones_cbf[:], 1.0)
        v.drain()
        vstat.tick(v.engine_nop())
    ONES_AT = vstat.n

    XLD_AT = {}

    def ln_gpsimd(g, i, src, row0, extra=None, bufs=None):
        bufs = bufs or xbuf
        par = i % 2
        if extra is not None:
            extra(g)
        if i >= 2:
            g.wait_ge(snorm.h, 2 * (i - 2) + 2)
        XLD_AT[i] = xld[par].dma(g, bufs[par][:], src[row0:row0 + 128, :])

    def ln_vec_ab(v, i, bufs=None):
        bufs = bufs or xbuf
        par = i % 2
        s = sml[par]
        v.wait_ge(xld[par].h, XLD_AT[i])
        for j in range(4):
            v.bn_stats(st6[par][:, j * 6:(j + 1) * 6],
                       bufs[par][:, j * 512:(j + 1) * 512])
        v.drain()
        v.bn_aggr(ag[par][:], st6[par][:])
        v.drain()
        v.tensor_scalar(s[:, 0:1], ag[par][:, 1:2], EPS, None, AluOpType.add)
        v.drain()
        v.reciprocal(s[:, 1:2], s[:, 0:1])
        v.drain()
        vstat.tick(v.engine_nop())
        v.wait_ge(snorm.h, 2 * i + 1)
        v.tensor_tensor(s[:, 2:3], s[:, 7:8], s[:, 7:8], AluOpType.mult)
        v.drain()
        v.tensor_tensor(s[:, 3:4], s[:, 0:1], s[:, 2:3], AluOpType.mult)
        v.drain()
        v.tensor_scalar(s[:, 4:5], s[:, 3:4], -0.5, 1.5,
                        AluOpType.mult, AluOpType.add)
        v.drain()
        v.tensor_tensor(s[:, 5:6], s[:, 7:8], s[:, 4:5], AluOpType.mult)
        v.drain()
        v.tensor_scalar(s[:, 6:7], ag[par][:, 0:1], -1.0, None, AluOpType.mult)
        v.drain()
        v.tensor_tensor(s[:, 2:3], s[:, 6:7], s[:, 5:6], AluOpType.mult)
        v.drain()
        vstat.tick(v.engine_nop())

    def ln_scalar(sc, i, outbuf, guards=(), bufs=None):
        bufs = bufs or xbuf
        par = i % 2
        sc.wait_ge(vstat.h, ONES_AT + 2 * i + 1)
        snorm.tick(sc.activation(sml[par][:, 7:8], sml[par][:, 1:2], AF.Sqrt))
        sc.wait_ge(vstat.h, ONES_AT + 2 * i + 2)
        for gs, gv in guards:
            if gv > 0:
                sc.wait_ge(gs.h, gv)
        snorm.tick(sc.activation(outbuf[:], bufs[par][:], AF.Identity,
                                 scale=sml[par][:, 5:6], bias=sml[par][:, 2:3]))

    def ln_block(nt, base, src, row_off, norm_out, norm_guards, g_extra=None,
                 g_post=None):
        @blk.gpsimd
        def _(g):
            for t in range(nt):
                ln_gpsimd(g, base + t, src, row_off + t * 128,
                          g_extra if t == 0 else None)
            if g_post is not None:
                g_post(g)

        @blk.vector
        def _(v):
            for t in range(nt):
                ln_vec_ab(v, base + t)

        @blk.scalar
        def _(sc):
            for t in range(nt):
                i = base + t
                ln_scalar(sc, i, norm_out(i), norm_guards(i))

    def emit_quant(v, srcbuf, F, amB, s2B, ivB, q8B, dqB):
        nb = F // 32
        src3 = srcbuf[:, 0:F].rearrange("p (g k) -> p g k", k=32)
        v.drain()
        v.tensor_reduce(amB[:, 0:nb], src3, axis=AX.X, op=AluOpType.max,
                        apply_absolute_value=True)
        v.drain()
        v.tensor_scalar(s2B[:, 0:nb], amB[:, 0:nb], AMAX_FLOOR, INV224,
                        AluOpType.max, AluOpType.mult)
        v.drain()
        v.reciprocal(ivB[:, 0:nb], s2B[:, 0:nb])
        v.drain()
        v.tensor_tensor(q8B[:, 0:F].rearrange("p (g k) -> p g k", k=32), src3,
                        ivB[:, 0:nb].unsqueeze(2).broadcast_to([128, nb, 32]),
                        AluOpType.mult)
        v.drain()
        v.tensor_tensor(dqB[:, 0:F].rearrange("p (g k) -> p g k", k=32),
                        q8B[:, 0:F].rearrange("p (g k) -> p g k", k=32),
                        s2B[:, 0:nb].unsqueeze(2).broadcast_to([128, nb, 32]),
                        AluOpType.mult)
        v.drain()

    NPREP = 192

    def prep_src(j):
        if j < 128:
            It = j // 2
            w = gw if j % 2 == 0 else uw
            return w[It * 128:(It + 1) * 128, :]
        n = j - 128
        return dw[(n // 4) * 128:(n // 4 + 1) * 128,
                  (n % 4) * 2048:(n % 4 + 1) * 2048]

    def prep_dst(j):
        if j < 128:
            It = j // 2
            w = gT_s if j % 2 == 0 else uT_s
            return w[It * 128:(It + 1) * 128, :]
        n = j - 128
        return dT_s[(n // 4) * 128:(n // 4 + 1) * 128,
                    (n % 4) * 2048:(n % 4 + 1) * 2048]

    P = {"ld": 0, "vp": 0, "dt": 0, "st": 0, "vps": 0}

    def prep_region():
        P["vps"] = P["vp"]

    def prep_gp(g, nloads, nstores, plan=0, tight=False):
        k = 0
        while P["st"] < min(P["dt"], NPREP) and k < nstores:
            j = P["st"]
            g.wait_ge(tprep.h, 16 * (j + 1))
            wsst.pdma(g, prep_dst(j), gbuf[j % 2][:])
            P["st"] += 1
            k += 1
        k = 0
        if tight:
            cap = min(2 * P["vps"] + 2, max(P["dt"] + 2, 2), NPREP)
        else:
            cap = min(2 * (P["vp"] + plan) + 2, max(P["st"] + 4, 4), NPREP)
        while P["ld"] < cap and k < nloads:
            j = P["ld"]
            if j >= 2:
                if tight:
                    g.wait_ge(tprep.h, 16 * (j - 1))
                g.wait_ge(gq.h, j - 1)
            wgld[j % 2].dma(g, wwb[j % 2][:], prep_src(j))
            P["ld"] += 1
            k += 1

    def prep_vec(v, npairs):
        k = 0
        while P["vp"] < min(P["ld"] // 2, NPREP // 2) and k < npairs:
            pk = P["vp"]
            a, b = 2 * pk, 2 * pk + 1
            v.wait_ge(wgld[0].h, 16 * (pk + 1))
            v.wait_ge(wgld[1].h, 16 * (pk + 1))
            if pk >= 1:
                v.wait_ge(tprep.h, 16 * (2 * pk))
            hb.tick(v.drain())
            sa = wwb[0][:].rearrange("p (g k) -> p g k", k=32)
            sbv = wwb[1][:].rearrange("p (g k) -> p g k", k=32)
            v.tensor_reduce(amq2[:, 0:64], sa, axis=AX.X, op=AluOpType.max,
                            apply_absolute_value=True)
            v.tensor_reduce(amq2[:, 64:128], sbv, axis=AX.X, op=AluOpType.max,
                            apply_absolute_value=True)
            hb.tick(v.drain())
            v.tensor_scalar(s2q2[:], amq2[:], AMAX_FLOOR, INV224,
                            AluOpType.max, AluOpType.mult)
            hb.tick(v.drain())
            v.reciprocal(ivq2[:], s2q2[:])
            hb.tick(v.drain())
            v.tensor_tensor(q8a[:].rearrange("p (g k) -> p g k", k=32), sa,
                            ivq2[:, 0:64].unsqueeze(2).broadcast_to([128, 64, 32]),
                            AluOpType.mult)
            v.tensor_tensor(q8b[:].rearrange("p (g k) -> p g k", k=32), sbv,
                            ivq2[:, 64:128].unsqueeze(2).broadcast_to([128, 64, 32]),
                            AluOpType.mult)
            hb.tick(v.drain())
            v.tensor_tensor(gdqb[:].rearrange("p (g k) -> p g k", k=32),
                            q8a[:].rearrange("p (g k) -> p g k", k=32),
                            s2q2[:, 0:64].unsqueeze(2).broadcast_to([128, 64, 32]),
                            AluOpType.mult)
            v.tensor_tensor(udqb[:].rearrange("p (g k) -> p g k", k=32),
                            q8b[:].rearrange("p (g k) -> p g k", k=32),
                            s2q2[:, 64:128].unsqueeze(2).broadcast_to([128, 64, 32]),
                            AluOpType.mult)
            hb.tick(v.drain())
            gq.tick(v.engine_nop())
            gq.tick(v.engine_nop())
            P["vp"] += 1
            k += 1

    def prep_sync(sy, nd, nloads=2):
        k = 0
        while P["dt"] < min(2 * P["vp"], P["st"] + 2, NPREP) and k < nd:
            j = P["dt"]
            sy.wait_ge(gq.h, j + 1)
            if j >= 2:
                sy.wait_ge(wsst.h, 16 * (j - 1))
            src = gdqb if j % 2 == 0 else udqb
            dmat(sy, tprep, gbuf[j % 2][:].rearrange("p (c d) -> p c d", d=128),
                 src[:])
            P["dt"] += 1
            k += 1


    HB1 = {}

    p1s = ExitStack()
    xnTh = sb(p1s, "xnTh", [128, HC, 1024], BF16)
    wkb = [sb(p1s, f"wkb{p}", [128, HC, 256], BF16) for p in range(2)]
    wvb = [sb(p1s, f"wvb{p}", [128, HC, 256], BF16) for p in range(2)]
    kevb = [sb(p1s, f"kevb{p}", [128, 512], BF16) for p in range(2)]
    vevb = [sb(p1s, f"vevb{p}", [128, 256], BF16) for p in range(2)]

    kst_cnt = [0]
    vst_cnt = [0]
    NDG = 8
    XNT_AT = {}

    def kv_wload(g, qq, dg):
        bg = qq * NDG + dg
        par = bg % 2
        if bg >= 2:
            g.wait_ge(kmm.h, (bg - 1) * 4)
            g.wait_ge(vmm.h, (bg - 1) * 8)
        kvwld[par].dma(g, wkb[par][:],
                       wk[:, dg * 256:(dg + 1) * 256]
                       .rearrange("(c p) d -> p c d", p=128))
        kvwld[par].dma(g, wvb[par][:],
                       wv[:, dg * 256:(dg + 1) * 256]
                       .rearrange("(c p) d -> p c d", p=128))

    for qq in range(4):
        base = qq * 8
        prep_region()

        @blk.gpsimd
        def _(g, qq=qq):
            for dg in range(2):
                kv_wload(g, qq, dg)
            prep_gp(g, 4, 0)

        ln_block(8, base, xb, qq * 1024,
                 norm_out=lambda i: xnbf[i % 2],
                 norm_guards=lambda i: [(tdone, 16 * (i - 1))])

        @blk.vector
        def _(v, qq=qq):
            prep_vec(v, 8)

        @blk.sync
        def _(sy, qq=qq):
            if qq >= 1:
                sy.wait_ge(kmm.h, 32 * qq)
                sy.wait_ge(vmm.h, 64 * qq)
            for t in range(8):
                i = qq * 8 + t
                sy.wait_ge(snorm.h, 2 * i + 2)
                XNT_AT[i] = dmat(sy, tdone,
                                 xnTh[:, :, t * 128:(t + 1) * 128],
                                 xnbf[i % 2][:])

        for dg in range(NDG):
            prep_region()

            @blk.gpsimd
            def _(g, qq=qq, dg=dg):
                for dtl in range(2):
                    dt = dg * 2 + dtl
                    for w in range(2):
                        kt = kst_cnt[0]
                        kst_cnt[0] += 1
                        g.wait_ge(kev.h, kt + 1)
                        kst[kt % 2].pdma(
                            g, kT_s[dt * 128:(dt + 1) * 128,
                                    qq * 1024 + w * 512:qq * 1024 + (w + 1) * 512],
                            kevb[kt % 2][:])
                for tt in range(8):
                    vt = vst_cnt[0]
                    vst_cnt[0] += 1
                    g.wait_ge(vev.h, vt + 1)
                    vst[vt % 2].pdma(
                        g, v_s[(qq * 8 + tt) * 128:(qq * 8 + tt + 1) * 128,
                               dg * 256:(dg + 1) * 256],
                        vevb[vt % 2][:])
                if dg + 2 < NDG:
                    kv_wload(g, qq, dg + 2)
                prep_gp(g, 4, 4, plan=2)

            @blk.vector
            def _(v, qq=qq, dg=dg):
                prep_vec(v, 1)
                HB1[(qq, dg)] = hb.n

            @blk.scalar
            def _(sc, qq=qq, dg=dg):
                for j in range(4):
                    kt = kev.n
                    sc.wait_ge(kmm.h, kt + 1)
                    if kt >= 2:
                        sc.wait_ge(kst[kt % 2].h, 16 * (kt // 2))
                    kev.tick(sc.activation(kevb[kt % 2][:], ps[kt % 2][:],
                                           AF.Copy))
                for j in range(8):
                    vt = vev.n
                    sc.wait_ge(vmm.h, vt + 1)
                    if vt >= 2:
                        sc.wait_ge(vst[vt % 2].h, 16 * (vt // 2))
                    vev.tick(sc.activation(vevb[vt % 2][:],
                                           ps[2 + vt % 2][:, 0:256], AF.Copy))

            @blk.sync
            def _(sy):
                prep_sync(sy, 4)

            @blk.gpsimd
            def _(g, qq=qq, dg=dg):
                prep_gp(g, 2, 2)

            @blk.sync
            def _(sy):
                prep_sync(sy, 2)

        @blk.tensor
        def _(te, qq=qq):
            if qq == 0:
                te.wait_ge(cst.h, 64)
                te.wait_ge(vstat.h, ONES_AT)
            for dg in range(NDG):
                bg = qq * NDG + dg
                par = bg % 2
                te.wait_ge(kvwld[par].h, 32 * (bg // 2 + 1))
                for dtl in range(2):
                    dt = dg * 2 + dtl
                    for w in range(2):
                        kt = kmm.n
                        bank = ps[kt % 2]
                        if kt >= 2:
                            te.wait_ge(kev.h, kt - 1)
                        te.wait_ge(tdone.h, XNT_AT[qq * 8 + 4 * (w + 1) - 1])
                        for hc in range(HC):
                            te.matmul(bank[:],
                                      wkb[par][:, hc, dtl * 128:(dtl + 1) * 128],
                                      xnTh[:, hc, w * 512:(w + 1) * 512],
                                      start=(hc == 0), stop=False)
                        kmm.tick(te.matmul(bank[:],
                                           bias_sb[1][0:1, dt * 128:(dt + 1) * 128],
                                           ones_bf[0:1, :],
                                           start=False, stop=True))
                for tt in range(8):
                    vt = vmm.n
                    bank = ps[2 + vt % 2]
                    if vt >= 2:
                        te.wait_ge(vev.h, vt - 1)
                    te.wait_ge(tdone.h, XNT_AT[qq * 8 + tt])
                    for hc in range(HC):
                        te.matmul(bank[:, 0:256],
                                  xnTh[:, hc, tt * 128:(tt + 1) * 128],
                                  wvb[par][:, hc, :],
                                  start=(hc == 0), stop=False)
                    vmm.tick(te.matmul(bank[:, 0:256],
                                       ones_bf[0:1, 0:128],
                                       bias_sb[2][0:1, dg * 256:(dg + 1) * 256],
                                       start=False, stop=True))


    prep_region()

    @blk.gpsimd
    def _(g):
        g.wait_ge(kmm.h, 128)
        g.wait_ge(vmm.h, 256)
        for dg in range(2):
            qwld[dg % 2].dma(g, wkb[dg % 2][:],
                             wq[:, dg * 256:(dg + 1) * 256]
                             .rearrange("(c p) d -> p c d", p=128))

    ln_block(8, 32, xo, 0,
             norm_out=lambda i: xnbf[i % 2],
             norm_guards=lambda i: [(tdone, 16 * (i - 1))])

    @blk.gpsimd
    def _(g):
        for dg in range(2, NDG):
            par = dg % 2
            g.wait_ge(qmm.h, (dg - 1) * 4)
            qwld[par].dma(g, wkb[par][:],
                          wq[:, dg * 256:(dg + 1) * 256]
                          .rearrange("(c p) d -> p c d", p=128))
        prep_gp(g, 8, 8, plan=2)

    @blk.vector
    def _(v):
        prep_vec(v, 6)

    @blk.sync
    def _(sy):
        sy.wait_ge(kmm.h, 128)
        sy.wait_ge(vmm.h, 256)
        for t in range(8):
            i = 32 + t
            sy.wait_ge(snorm.h, 2 * i + 2)
            XNT_AT[i] = dmat(sy, tdone, xnTh[:, :, t * 128:(t + 1) * 128],
                             xnbf[i % 2][:])
        prep_sync(sy, 6)

    @blk.tensor
    def _(te):
        for dg in range(NDG):
            par = dg % 2
            te.wait_ge(qwld[par].h, 16 * (dg // 2 + 1))
            for dtl in range(2):
                dt = dg * 2 + dtl
                for w in range(2):
                    qt = qmm.n
                    bank = ps[4 + qt % 2]
                    if qt >= 2:
                        te.wait_ge(qev.h, qt - 1)
                    te.wait_ge(tdone.h, XNT_AT[32 + 4 * (w + 1) - 1])
                    for hc in range(HC):
                        te.matmul(bank[:],
                                  wkb[par][:, hc, dtl * 128:(dtl + 1) * 128],
                                  xnTh[:, hc, w * 512:(w + 1) * 512],
                                  start=(hc == 0), stop=False)
                    qmm.tick(te.matmul(bank[:],
                                       bias_sb[0][0:1, dt * 128:(dt + 1) * 128],
                                       ones_bf[0:1, :],
                                       start=False, stop=True))

    @blk.scalar
    def _(sc):
        qcnt = [0]
        for dg in range(NDG):
            for dtl in range(2):
                dt = dg * 2 + dtl
                for w in range(2):
                    qt = qcnt[0]
                    qcnt[0] += 1
                    sc.wait_ge(qmm.h, qt + 1)
                    qev.tick(sc.activation(QT[:, dt, w * 512:(w + 1) * 512],
                                           ps[4 + qt % 2][:], AF.Copy))

    p1s.close()
    xnbfS.close()

    p2s = ExitStack()
    krow = [[sb(p2s, f"krow{s_}_{i}", [128, S], BF16) for i in range(2)]
            for s_ in range(2)]
    vchunk = [sb(p2s, f"vchunk{p}", [128, 2 * 128], BF16) for p in range(4)]
    PT = [sb(p2s, f"PT{p}", [128, 1024], BF16) for p in range(2)]
    izv = sb(p2s, "izv", [1, 1024], F32)
    izb = sb(p2s, "izb", [128, 1024], F32)

    def krld_target(p):
        st = p % 2
        return 16 * sum(GLEN[r] for r in range(p + 1) if r % 2 == st)

    AVCUM = []
    for p, hs in enumerate(PASSES):
        for kc in range(32):
            AVCUM.append(2 * (PT0[p] + (kc + 1) * len(hs)))

    for p, hs in enumerate(PASSES):
        prep_region()

        for sub in range(8):
            prep_region()

            @blk.gpsimd
            def _(g, p=p, hs=hs, sub=sub):
                st = p % 2
                if p == 0 and sub == 0:
                    g.wait_ge(qev.h, 32)
                    for pp in range(2):
                        g.wait_ge(kst[pp].h, kst[pp].n)
                        g.wait_ge(vst[pp].h, vst[pp].n)
                if sub == 0:
                    if p >= 2:
                        g.wait_ge(smm.h, 2 * PT0[p - 1])
                    for i, h in enumerate(hs):
                        krld[st].dma(g, krow[st][i][:],
                                     kT_s[h * 128:(h + 1) * 128, :])
                kcs = list(range(sub * 4, sub * 4 + 4)) if sub == 0 else []
                if sub < 7:
                    kcs += list(range((sub + 1) * 4, (sub + 1) * 4 + 4))
                for kc in kcs:
                    n = p * 32 + kc
                    if n >= 4:
                        g.wait_ge(ptav.h, AVCUM[n - 4])
                    d0 = hs[0] * 128
                    vcld[n % 4].dma(g, vchunk[n % 4][:, 0:len(hs) * 128],
                                    v_s[kc * 128:(kc + 1) * 128,
                                        d0:d0 + len(hs) * 128])
                prep_gp(g, 4, 4, plan=2)

            if sub % 2 == 1:
                @blk.vector
                def _(v, p=p, sub=sub):
                    prep_vec(v, 1)

            @blk.sync
            def _(sy, p=p, sub=sub):
                prep_sync(sy, 4)

        @blk.tensor
        def _(te, p=p, hs=hs):
            st = p % 2
            gl = len(hs)
            if p == 0:
                te.wait_ge(qev.h, 32)
            te.wait_ge(krld[st].h, krld_target(p))
            te.wait_ge(oev.h, 2 * GH0[p])
            for kc in range(32):
                n = p * 32 + kc
                te.wait_ge(vcld[n % 4].h, 16 * (n // 4 + 1))
                for i in range(gl):
                    ptn = PT0[p] + kc * gl + i
                    for qh in range(2):
                        scn = 2 * ptn + qh
                        if scn >= 2:
                            te.wait_ge(sexp.h, scn - 1)
                        smm.tick(te.matmul(ps[qh][:],
                                           krow[st][i][:, kc * 128:(kc + 1) * 128],
                                           QT[:, hs[i], qh * 512:(qh + 1) * 512],
                                           start=True, stop=True))
                for i in range(gl):
                    ptn = PT0[p] + kc * gl + i
                    for qh in range(2):
                        te.wait_ge(sexp.h, 2 * ptn + qh + 1)
                        te.matmul(ps[2 + 2 * i + qh][:],
                                  vchunk[n % 4][:, i * 128:(i + 1) * 128],
                                  PT[ptn % 2][:, qh * 512:(qh + 1) * 512],
                                  start=(kc == 0), stop=(kc == 31))
                        ptav.tick(te.matmul(
                            ps[6 + qh][32 * i:32 * i + 1, :], ones_cbf[:],
                            PT[ptn % 2][:, qh * 512:(qh + 1) * 512],
                            start=(kc == 0), stop=(kc == 31)))
            te.wait_ge(sexp.h, 2 * PT0[p + 1])
            for i in range(gl):
                gh = GH0[p] + i
                if i > 0:
                    te.wait_ge(oev.h, 2 * gh)
                te.wait_ge(vzf.h, gh + 1)
                for qh in range(2):
                    bcmm.tick(te.matmul(ps[qh][:], ones_row[0:1, 0:128],
                                        izv[0:1, qh * 512:(qh + 1) * 512],
                                        start=True, stop=True))

        @blk.scalar
        def _(sc, p=p, hs=hs):
            gl = len(hs)
            for kc in range(32):
                for i in range(gl):
                    ptn = PT0[p] + kc * gl + i
                    for qh in range(2):
                        scn = 2 * ptn + qh
                        sc.wait_ge(smm.h, scn + 1)
                        if ptn >= 2:
                            sc.wait_ge(ptav.h, 2 * (ptn - 1))
                        sexp.tick(sc.activation(
                            PT[ptn % 2][:, qh * 512:(qh + 1) * 512],
                            ps[qh][:], AF.Exp, scale=SCL))

        @blk.vector
        def _(v, p=p, hs=hs):
            gl = len(hs)
            for i in range(gl):
                gh = GH0[p] + i
                v.wait_ge(bcmm.h, 2 * gh)
                v.wait_ge(ptav.h, 2 * (PT0[p] + 31 * gl + i + 1))
                v.reciprocal(izv[0:1, 0:512], ps[6][32 * i:32 * i + 1, :])
                v.reciprocal(izv[0:1, 512:1024], ps[7][32 * i:32 * i + 1, :])
                vzf.tick(v.drain())
                v.wait_ge(bcmm.h, 2 * gh + 2)
                v.tensor_copy(izb[:, 0:512], ps[0][:])
                v.tensor_copy(izb[:, 512:1024], ps[1][:])
                v.drain()
                for qh in range(2):
                    v.tensor_tensor(AVT[:, hs[i], qh * 512:(qh + 1) * 512],
                                    ps[2 + 2 * i + qh][:],
                                    izb[:, qh * 512:(qh + 1) * 512],
                                    AluOpType.mult)
                    oev.tick(v.drain())

        @blk.vector
        def _(v, p=p):
            prep_vec(v, 2)

    p2s.close()
    QTs.close()

    p3s = ExitStack()
    woblk = [sb(p3s, f"woblk{og}", [128, HC, 512], BF16) for og in range(4)]
    outb = [sb(p3s, f"outb{p}", [128, H], F32) for p in range(2)]
    bo_sb = sb(p3s, "bo_sb", [1, H], BF16)
    bost = sem("bost")

    XLD3 = []
    _x0, _x1 = xld[0].n, xld[1].n
    for tt in range(8):
        if tt % 2 == 0:
            _x0 += 16
            XLD3.append(_x0)
        else:
            _x1 += 16
            XLD3.append(_x1)

    for tt in range(8):
        prep_region()

        @blk.gpsimd
        def _(g, tt=tt):
            par = tt % 2
            if tt == 0:
                g.wait_ge(oev.h, 32)
                bost.dma(g, bo_sb[:], bias4[3:4, :])
                for og in range(4):
                    wowld.dma(g, woblk[og][:],
                              wo[:, og * 512:(og + 1) * 512]
                              .rearrange("(c p) d -> p c d", p=128))
                g.wait_ge(snorm.h, 80)
            if tt >= 2:
                g.wait_ge(p3ev.h, 4 * (tt - 1))
            xld[par].dma(g, xbuf[par][:], xo[tt * 128:(tt + 1) * 128, :])
            if tt >= 2:
                g.wait_ge(p3ev.h, 4 * (tt - 1))
                ost[tt % 2].pdma(g, out[(tt - 2) * 128:(tt - 1) * 128, :],
                                 outb[tt % 2][:])
            prep_gp(g, 4, 4, plan=2)

        @blk.vector
        def _(v, tt=tt):
            par = tt % 2
            v.wait_ge(xld[par].h, XLD3[tt])
            if tt >= 2:
                v.wait_ge(ost[par].h, 16 * (tt // 2))
            for og in range(4):
                nn = tt * 4 + og
                v.wait_ge(p3mm.h, nn + 1)
                v.drain()
                v.tensor_tensor(outb[par][:, og * 512:(og + 1) * 512],
                                ps[nn % 2][:],
                                xbuf[par][:, og * 512:(og + 1) * 512],
                                AluOpType.add)
                p3ev.tick(v.drain())
            prep_vec(v, 1)

        @blk.gpsimd
        def _(g, tt=tt):
            prep_gp(g, 2, 2, plan=1)

        @blk.vector
        def _(v, tt=tt):
            prep_vec(v, 1)

        @blk.sync
        def _(sy, tt=tt):
            prep_sync(sy, 6, 4)

    @blk.gpsimd
    def _(g):
        for tt in range(6, 8):
            g.wait_ge(p3ev.h, 4 * (tt + 1))
            ost[tt % 2].pdma(g, out[tt * 128:(tt + 1) * 128, :], outb[tt % 2][:])

    @blk.tensor
    def _(te):
        te.wait_ge(oev.h, 32)
        te.wait_ge(wowld.h, 64)
        te.wait_ge(bost.h, 16)
        for tt in range(8):
            for og in range(4):
                nn = tt * 4 + og
                bank = ps[nn % 2]
                if nn >= 2:
                    te.wait_ge(p3ev.h, nn - 1)
                for hc in range(HC):
                    te.matmul(bank[:],
                              AVT[:, hc, tt * 128:(tt + 1) * 128],
                              woblk[og][:, hc, :],
                              start=(hc == 0), stop=False)
                p3mm.tick(te.matmul(bank[:],
                                    ones_bf[0:1, 0:128],
                                    bo_sb[0:1, og * 512:(og + 1) * 512],
                                    start=False, stop=True))

    @blk.sync
    def _(sy):
        prep_sync(sy, 999)


    p3s.close()
    AVTs.close()
    xbufS.close()
    biasS.close()

    p4s = ExitStack()
    xdqT = sb(p4s, "xdqT", [128, HC, TO], BF16)
    xwb = [sb(p4s, f"xwb{p}", [128, H], F32) for p in range(2)]
    xdq = [sb(p4s, f"xdq{p}", [128, H], BF16) for p in range(2)]
    gbufL = [sb(p4s, f"gbufL{p}", [128, H], BF16) for p in range(2)]
    ubufL = [sb(p4s, f"ubufL{p}", [128, H], BF16) for p in range(2)]
    sgb = [sb(p4s, f"sgb{p}", [128, 512], F32) for p in range(2)]
    hfb = sb(p4s, "hfb", [128, 512], F32)
    htt = sb(p4s, "htt", [128, 512], F32)
    amh = sb(p4s, "amh", [128, 16], F32)
    s2h = sb(p4s, "s2h", [128, 16], F32)
    ivh = sb(p4s, "ivh", [128, 16], F32)
    h8b = sb(p4s, "h8b", [128, 512], FP8)
    hdt = sb(p4s, "hdt", [128, 512], BF16)
    hdq = sb(p4s, "hdq", [128, 64 * 512], BF16)
    dbufp = [sb(p4s, f"dbufp{p}", [128, H], BF16) for p in range(2)]
    ftb = [sb(p4s, f"ftb{p}", [128, 512], F32) for p in range(2)]
    tbuf = [sb(p4s, f"tbuf{p}", [128, 128], F32) for p in range(2)]

    TD_P4A = tdone.n

    @blk.gpsimd
    def _(g):
        for t in range(8):
            i = 40 + t
            par = i % 2
            if t == 0:
                g.wait_ge(p3ev.h, 32)
                g.wait_ge(p3mm.h, 32)
                for pp in range(2):
                    g.wait_ge(ost[pp].h, ost[pp].n)
            if t >= 2:
                g.wait_ge(xq.h, t - 1)
            XLD_AT[i] = xld[par].dma(g, xwb[par][:], xo[t * 128:(t + 1) * 128, :])
            prep_gp(g, 2, 2)

    def p4a_quant(v, t):
        par = (40 + t) % 2
        v.wait_ge(snorm.h, 2 * (40 + t) + 2)
        if t >= 2:
            v.wait_ge(tdone.h, TD_P4A + 16 * (t - 1))
        emit_quant(v, xwb[par], H, amq2, s2q2, ivq2, q8a, xdq[par])
        xq.tick(v.engine_nop())

    @blk.vector
    def _(v):
        for t in range(8):
            prep_region()
            ln_vec_ab(v, 40 + t, bufs=xwb)
            if t >= 1:
                p4a_quant(v, t - 1)
        p4a_quant(v, 7)

    @blk.scalar
    def _(sc):
        for t in range(8):
            i = 40 + t
            ln_scalar(sc, i, xwb[i % 2], bufs=xwb)

    @blk.sync
    def _(sy):
        for t in range(8):
            sy.wait_ge(xq.h, t + 1)
            dmat(sy, tdone, xdqT[:, :, t * 128:(t + 1) * 128], xdq[t % 2][:])
            prep_sync(sy, 3)

    NCH = 4

    def emit_hpipe(v, th, It):
        gg = th * 64 + It
        v.wait_ge(hsil.h, gg + 1)
        v.drain()
        v.tensor_tensor(hfb[:], sgb[gg % 2][:], ps[2 + gg % 2][:],
                        AluOpType.mult)
        v.drain()
        v.transpose(htt[:], hfb[:])
        v.drain()
        ht3 = htt[:].rearrange("p (g k) -> p g k", k=32)
        v.tensor_reduce(amh[:], ht3, axis=AX.X, op=AluOpType.max,
                        apply_absolute_value=True)
        v.drain()
        v.tensor_scalar(s2h[:], amh[:], AMAX_FLOOR, INV224,
                        AluOpType.max, AluOpType.mult)
        v.drain()
        v.reciprocal(ivh[:], s2h[:])
        v.drain()
        v.tensor_tensor(h8b[:].rearrange("p (g k) -> p g k", k=32), ht3,
                        ivh[:].unsqueeze(2).broadcast_to([128, 16, 32]),
                        AluOpType.mult)
        v.drain()
        v.tensor_tensor(hdt[:].rearrange("p (g k) -> p g k", k=32),
                        h8b[:, 0:512].rearrange("p (g k) -> p g k", k=32),
                        s2h[:].unsqueeze(2).broadcast_to([128, 16, 32]),
                        AluOpType.mult)
        v.drain()
        v.transpose(hdq[:, It * 512:(It + 1) * 512], hdt[:])
        hq.tick(v.drain())

    def emit_tcopy(v, th, gg):
        for tq in range(4):
            m = 4 * gg + tq
            v.wait_ge(ttr.h, 4 * gg + 4)
            if m >= 2:
                v.wait_ge(accst[m % 2].h, 16 * (m // 2))
            v.tensor_copy(tbuf[m % 2][:],
                          ps[6 + gg % 2][:, tq * 128:(tq + 1) * 128])
            tbe.tick(v.drain())

    def emit_accums(g, gg):
        for tq in range(4):
            m = 4 * gg + tq
            g.wait_ge(tbe.h, m + 1)
            th_, ho_ = gg // 16, gg % 16
            accst[m % 2].pdma(
                g, out[(th_ * 4 + tq) * 128:(th_ * 4 + tq + 1) * 128,
                       ho_ * 128:(ho_ + 1) * 128],
                tbuf[m % 2][:], accum_op=AluOpType.add)

    for th in range(2):
        for It in range(64):
            prep_region()

            @blk.gpsimd
            def _(g, th=th, It=It):
                par = It % 2
                gg = th * 64 + It
                prep_gp(g, 2, 2, tight=True)
                assert P["st"] >= 2 * It + 2, (P, th, It)
                if gg >= 2:
                    g.wait_ge(p4mm.h, gg - 1)
                g.wait_ge(wsst.h, 16 * (2 * It + 2))
                gsld[par].dma(g, gbufL[par][:], gT_s[It * 128:(It + 1) * 128, :])
                gsld[par].dma(g, ubufL[par][:], uT_s[It * 128:(It + 1) * 128, :])

            @blk.vector
            def _(v, th=th, It=It):
                if th == 1 and It == 0:
                    v.wait_ge(p4dpc.h, 64)
                emit_hpipe(v, th, It)
                prep_vec(v, 1)

            @blk.sync
            def _(sy, th=th, It=It):
                prep_sync(sy, 2)

        @blk.tensor
        def _(te, th=th):
            if th == 0:
                te.wait_ge(p3ev.h, 32)
                te.wait_ge(tdone.h, TD_P4A + 16 * 8)
            for It in range(64):
                par = It % 2
                gg = th * 64 + It
                gb = gbufL[par][:].rearrange("p (c d) -> p c d", d=128)
                ub = ubufL[par][:].rearrange("p (c d) -> p c d", d=128)
                te.wait_ge(gsld[par].h, 32 * (th * 32 + It // 2 + 1))
                if gg >= 2:
                    te.wait_ge(hsil.h, gg - 1)
                    te.wait_ge(hq.h, gg - 1)
                for hc in range(HC):
                    te.matmul(ps[gg % 2][:], gb[:, hc, :],
                              xdqT[:, hc, th * 512:(th + 1) * 512],
                              start=(hc == 0), stop=(hc == HC - 1))
                for hc in range(HC):
                    ii = te.matmul(ps[2 + gg % 2][:], ub[:, hc, :],
                                   xdqT[:, hc, th * 512:(th + 1) * 512],
                                   start=(hc == 0), stop=(hc == HC - 1))
                p4mm.tick(ii)

        @blk.scalar
        def _(sc, th=th):
            for It in range(64):
                gg = th * 64 + It
                sc.wait_ge(p4mm.h, gg + 1)
                if gg >= 2:
                    sc.wait_ge(hq.h, gg - 1)
                hsil.tick(sc.activation(sgb[gg % 2][:], ps[gg % 2][:], AF.Silu))

        for ho in range(16):
            prep_region()

            @blk.gpsimd
            def _(g, th=th, ho=ho):
                prep_gp(g, 4, 4, tight=True)
                for ch in range(NCH):
                    n = ho * NCH + ch
                    gp = th * 64 + n
                    if gp >= 2:
                        g.wait_ge(p4dpc.h, gp - 1)
                    if th == 0:
                        assert P["st"] >= 129 + n, (P, th, ho, ch, n)
                        g.wait_ge(wsst.h, 16 * (129 + n))
                    dsld[gp % 2].dma(g, dbufp[gp % 2][:],
                                     dT_s[ho * 128:(ho + 1) * 128,
                                          ch * 2048:(ch + 1) * 2048])
                if ho >= 2:
                    emit_accums(g, th * 16 + ho - 2)
                if ho == 15:
                    for gg in range(th * 16 + 14, th * 16 + 16):
                        emit_accums(g, gg)

            @blk.vector
            def _(v, th=th, ho=ho):
                emit_tcopy(v, th, th * 16 + ho)
                prep_vec(v, 1)

            @blk.sync
            def _(sy, th=th, ho=ho):
                prep_sync(sy, 4, 4)

            @blk.gpsimd
            def _(g, th=th, ho=ho):
                prep_gp(g, 2, 2, tight=True)

            @blk.vector
            def _(v, th=th, ho=ho):
                prep_vec(v, 1)

            @blk.sync
            def _(sy, th=th, ho=ho):
                prep_sync(sy, 4, 4)

        @blk.tensor
        def _(te, th=th):
            te.wait_ge(hq.h, 64 * (th + 1))

            def down_mm(ho):
                gg = th * 16 + ho
                if gg >= 2:
                    te.wait_ge(fts.h, gg - 1)
                for ch in range(NCH):
                    n = ho * NCH + ch
                    gp = th * 64 + n
                    te.wait_ge(dsld[gp % 2].h, 16 * (gp // 2 + 1))
                    for icl in range(16):
                        ic = ch * 16 + icl
                        ii = te.matmul(ps[4 + gg % 2][:],
                                       dbufp[gp % 2][:, icl * 128:(icl + 1) * 128],
                                       hdq[:, ic * 512:(ic + 1) * 512],
                                       start=(ic == 0), stop=(ic == 63))
                    p4dpc.tick(ii)

            def transp(ho):
                gg = th * 16 + ho
                te.wait_ge(fts.h, gg + 1)
                if gg >= 2:
                    te.wait_ge(tbe.h, 4 * (gg - 1))
                for tq in range(4):
                    ttr.tick(te.transpose(
                        ps[6 + gg % 2][:, tq * 128:(tq + 1) * 128],
                        ftb[gg % 2][:, tq * 128:(tq + 1) * 128],
                        ident_sb[:]))

            for ho in range(16):
                down_mm(ho)
                if ho >= 2:
                    transp(ho - 2)
            transp(14)
            transp(15)

        @blk.scalar
        def _(sc, th=th):
            for ho in range(16):
                gg = th * 16 + ho
                sc.wait_ge(p4dpc.h, 4 * gg + 4)
                if gg >= 2:
                    sc.wait_ge(ttr.h, 4 * (gg - 1))
                fts.tick(sc.activation(ftb[gg % 2][:], ps[4 + gg % 2][:], AF.Copy))

    assert P["ld"] == NPREP and P["vp"] * 2 == NPREP
    assert P["dt"] == NPREP and P["st"] == NPREP

    @blk.gpsimd
    def _(g):
        for pp in range(2):
            g.wait_ge(accst[pp].h, accst[pp].n)
            g.wait_ge(ost[pp].h, ost[pp].n)

    p4s.close()
    prepS.close()
    es.close()
    return nc


def kernel(L0_attn_hidden_state, L0_attn_Wq, L0_attn_bq, L0_attn_Wk, L0_attn_bk,
           L0_attn_Wv, L0_attn_bv, L0_attn_Wo, L0_attn_bo,
           L0_ffn_gate_w, L0_ffn_up_w, L0_ffn_down_w):
    x = np.ascontiguousarray(np.asarray(L0_attn_hidden_state, dtype=np.float32))
    Wq = np.ascontiguousarray(np.asarray(L0_attn_Wq, dtype=np.float32))
    Wk = np.ascontiguousarray(np.asarray(L0_attn_Wk, dtype=np.float32))
    Wv = np.ascontiguousarray(np.asarray(L0_attn_Wv, dtype=np.float32))
    Wo = np.ascontiguousarray(np.asarray(L0_attn_Wo, dtype=np.float32))
    bias = np.ascontiguousarray(np.stack([
        np.asarray(L0_attn_bq, dtype=np.float32),
        np.asarray(L0_attn_bk, dtype=np.float32),
        np.asarray(L0_attn_bv, dtype=np.float32),
        np.asarray(L0_attn_bo, dtype=np.float32)]))
    Gw = np.ascontiguousarray(np.asarray(L0_ffn_gate_w, dtype=np.float32))
    Uw = np.ascontiguousarray(np.asarray(L0_ffn_up_w, dtype=np.float32))
    Dw = np.ascontiguousarray(np.asarray(L0_ffn_down_w, dtype=np.float32))
    ident = np.eye(128, dtype=np.float32)

    if "nc" not in _CACHE:
        _CACHE["nc"] = build_program()
    nc = _CACHE["nc"]

    in_maps = []
    for c in range(8):
        b, sl = c // 4, c % 4
        in_maps.append({
            "xb": x[b],
            "xo": np.ascontiguousarray(x[b, sl * TO:(sl + 1) * TO]),
            "wq": Wq, "wk": Wk, "wv": Wv, "wo": Wo,
            "bias4": bias, "gw": Gw, "uw": Uw, "dw": Dw,
            "ident": ident,
        })
    res = run_bass_kernel_spmd(nc, in_maps, core_ids=list(range(8)))
    outp = np.empty((2, S, H), dtype=np.float32)
    for c in range(8):
        b, sl = c // 4, c % 4
        outp[b, sl * TO:(sl + 1) * TO] = np.asarray(res.results[c]["out"])
    return outp
